# version 16
# baseline (speedup 1.0000x reference)
import numpy as np
import ml_dtypes
import concourse.bass as bass
import concourse.mybir as mybir
from concourse.bass_utils import run_bass_kernel_spmd

F32 = mybir.dt.float32
I32 = mybir.dt.int32
BF16 = mybir.dt.bfloat16
I16 = mybir.dt.int16
U8 = mybir.dt.uint8
AX = mybir.AxisListType.X
OP = mybir.AluOpType
AF = mybir.ActivationFunctionType

T, D, H, HD = 2048, 1024, 16, 64
I = 4096
C = 128
NCH = T // C
NT = T // 128
KD = D // 128
KI = I // 128
NP = H // 2
HALF = I // 2

ENGS = ["sync", "gpsimd", "scalar", "tensor", "vector"]


class Ctx:

    def __init__(self, sems, eng_name, eng):
        self.sems = sems
        self.eng_name = eng_name
        self.eng = eng
        self.cnt = {k: 0 for k in sems}

    def val(self, sem):
        return self.cnt[sem]

    def on(self, e, build, waits=(), incs=()):
        if e == self.eng_name:
            for sem, v in waits:
                if callable(v):
                    v = v()
                if v > 0:
                    self.eng.wait_ge(self.sems[sem], v)
            inst = build(self.eng)
            for sem, amt in incs:
                inst = inst.then_inc(self.sems[sem], amt)
        for sem, amt in incs:
            self.cnt[sem] += amt

    def dma(self, e, build, sem, waits=()):
        self.on(e, build, waits=list(waits) + [(sem, self.cnt[sem])], incs=[(sem, 16)])

    def wait(self, e, waits):
        if e == self.eng_name:
            for sem, v in waits:
                if callable(v):
                    v = v()
                if v > 0:
                    self.eng.wait_ge(self.sems[sem], v)


def build_nc():
    nc = bass.Bass("TRN2", target_bir_lowering=False, detect_race_conditions=False)
    x_tm_d = nc.dram_tensor("x_tm", [T, D], F32, kind="ExternalInput").ap()
    x_fm_d = nc.dram_tensor("x_fm", [D, T], F32, kind="ExternalInput").ap()
    wkT_d = nc.dram_tensor("wkT", [D, D], F32, kind="ExternalInput").ap()
    wvT_d = nc.dram_tensor("wvT", [D, D], F32, kind="ExternalInput").ap()
    wrT_d = nc.dram_tensor("wrT", [D, D], F32, kind="ExternalInput").ap()
    woT_d = nc.dram_tensor("woT", [D, D], F32, kind="ExternalInput").ap()
    wgT_d = nc.dram_tensor("wgT", [D, I], F32, kind="ExternalInput").ap()
    wuT_d = nc.dram_tensor("wuT", [D, I], F32, kind="ExternalInput").ap()
    wdT_d = nc.dram_tensor("wdT", [I, D], F32, kind="ExternalInput").ap()
    lam_d = nc.dram_tensor("lam", [128, KD], F32, kind="ExternalInput").ap()
    tf_d = nc.dram_tensor("tf", [128, KD], F32, kind="ExternalInput").ap()
    idf_d = nc.dram_tensor("idf", [128, 128], F32, kind="ExternalInput").ap()
    idb_d = nc.dram_tensor("idb", [128, 128], BF16, kind="ExternalInput").ap()
    mdiag_d = nc.dram_tensor("mdiag", [128, 128], F32, kind="ExternalInput").ap()
    m16_d = nc.dram_tensor("m16", [128, 128], U8, kind="ExternalInput").ap()
    m32_d = nc.dram_tensor("m32", [128, 128], U8, kind="ExternalInput").ap()
    m64_d = nc.dram_tensor("m64", [128, 128], U8, kind="ExternalInput").ap()
    out_d = nc.dram_tensor("out", [T, D], F32, kind="ExternalOutput").ap()
    x2_scr = nc.dram_tensor("x2scr", [T, D], F32, kind="Internal").ap()
    q2_scr = nc.dram_tensor("q2scr", [NT, 128, I], BF16, kind="Internal").ap()

    arena_h = nc.alloc_sbuf_tensor("arena", [128, 212736], U8)
    BASE = nc.lookup_mloc(arena_h).addr

    def place(name, shape, dt, off):
        return nc.alloc_sbuf_tensor_at(name, shape, dt, offset=BASE + off)

    wk = place("wk", [128, KD * D], F32, 0)
    wv = place("wv", [128, KD * D], F32, 32768)
    wr = place("wr", [128, KD * D], F32, 65536)
    wo = place("wo", [128, KD * D], F32, 98304)
    wgq = place("wgq", [128, KD * I], BF16, 0)
    wuq = place("wuq", [128, KD * I], BF16, 65536)
    wdq = place("wdq", [128, KI * D], BF16, 0)
    cb = 131072
    idf = place("idf", [128, 128], F32, cb); cb += 512
    identb = place("identb", [128, 128], BF16, cb); cb += 256
    mdiag = place("mdiag", [128, 128], F32, cb); cb += 512
    m16 = place("m16", [128, 128], U8, cb); cb += 128
    m32 = place("m32", [128, 128], U8, cb); cb += 128
    m64 = place("m64", [128, 128], U8, cb); cb += 128
    ones_col = place("ones_col", [128, 1], F32, cb); cb += 32
    ones_row = place("ones_row", [1, 128], F32, cb); cb += 512
    ones128 = place("ones128", [128, 128], F32, cb); cb += 512
    lam = place("lam", [128, KD], F32, cb); cb += 32
    tfb = place("tfb", [128, KD], F32, cb); cb += 32
    mu = place("mu", [128, KD], F32, cb); cb += 32
    tMU = place("tMU", [128, KD * 16], F32, cb); cb += 512
    srow = place("srow", [1, 256], F32, cb); cb += 1024
    epsc = place("epsc", [128, 1], F32, cb); cb += 32
    tmp1 = place("tmp1", [1, 128], F32, cb); cb += 512
    thr = place("thr", [128, 9], F32, cb); cb += 64
    acc = place("acc", [128, 24], F32, cb); cb += 96
    red1 = place("red1", [128, 3], F32, cb); cb += 32
    mrow = place("mrow", [1, 9], F32, cb); cb += 64
    sm = place("sm", [128, 24], F32, cb); cb += 96
    s2col = place("s2col", [128, NT], F32, cb); cb += 64
    zm2 = place("zm2", [128, 2], F32, cb); cb += 32
    assert cb <= 136448, cb
    AR = 136448
    TOPB = 212736
    assert AR + 76288 <= TOPB

    a = AR
    tRS = place("tRS", [128, KD * 128], F32, a); a += 4096
    tR16 = place("tR16", [128, KD * 128], F32, a); a += 4096
    tR32 = place("tR32", [128, KD * 128], F32, a); a += 4096
    tR64 = place("tR64", [128, KD * 128], F32, a); a += 4096
    tK16d = place("tK16d", [128, KD * 128], F32, a); a += 4096
    tK16 = place("tK16", [128, KD * 128], F32, a); a += 4096
    tK32 = place("tK32", [128, KD * 128], F32, a); a += 4096
    tK64 = place("tK64", [128, KD * 128], F32, a); a += 4096
    tKS = place("tKS", [128, KD * 128], F32, a); a += 4096
    fF = place("fF", [128, KD * 128], F32, a); a += 4096
    x_fmt1 = place("x_fmt1", [128, KD * 128], F32, nc.lookup_mloc(fF).addr - BASE)
    S_all = place("S_all", [128, NP * 64], F32, a); a += 2048
    x_fmt0 = place("x_fmt0", [128, KD * 128], F32, a); a += 4096
    x_tmt = place("x_tmt", [128, D], F32, a); a += 4096
    k_fm = place("k_fm", [128, KD * 128], F32, a); a += 4096
    r_fm = place("r_fm", [128, KD * 128], F32, a); a += 4096
    y_fm = place("y_fm", [128, KD * 128], F32, a)
    v_fm = place("v_fm", [128, KD * 128], F32, a); a += 4096
    x2_sb = place("x2_sb", [128, D], F32, nc.lookup_mloc(k_fm).addr - BASE)
    sq_fm = place("sq_fm", [128, KD * 128], F32, a); a += 4096
    pvar = place("pvar", [128, 9 * 128], F32, a); a += 4608
    v_tm = place("v_tm", [128, 128], F32, a); a += 512
    ks_tm = place("ks_tm", [128, 128], F32, a); a += 512
    A_sb = []
    for j in range(2):
        A_sb.append(place(f"A_sb{j}", [128, 128], F32, a)); a += 512
    assert a <= TOPB, a

    stag = [place(f"stag{j}", [128, I], F32, AR + 16384 * j) for j in range(2)]
    wqa = place("wqa", [128, I], BF16, AR + 32768)
    wqb = place("wqb", [128, I], BF16, AR + 40960)

    bA = AR
    x2t = [place(f"x2t{j}", [128, D], F32, bA + 4096 * j) for j in range(2)]
    sqt = place("sqt", [128, D], F32, bA + 8192)
    xnt = place("xnt", [128, D], F32, bA + 12288)
    u1t = place("u1t", [128, D], F32, bA + 16384)
    fl1 = place("fl1", [128, D], I32, bA + 20480)
    t1a = place("t1a", [128, D], BF16, bA + 24576)
    t1b = place("t1b", [128, D], BF16, bA + 26624)
    q1tm = place("q1tm", [128, D], BF16, bA + 28672)
    q1fm = place("q1fm", [128, D], BF16, bA + 30720)
    gt = place("gt", [128, HALF], F32, bA + 32768)
    ut = place("ut", [128, HALF], F32, bA + 40960)
    zt = place("zt", [128, I], F32, bA + 49152)
    u2t = place("u2t", [128, HALF], F32, bA + 12288)
    fl2 = place("fl2", [128, HALF], I16, bA + 8192)
    t2a = place("t2a", [128, HALF], BF16, bA + 20480)
    q2tm = place("q2tm", [128, HALF], BF16, bA + 24576)
    q2pp = [place(f"q2pp{j}", [128, 1024], BF16, bA + 65536 + 2048 * j) for j in range(2)]
    q2tm2 = place("q2tm2", [128, HALF], BF16, bA + 69632)
    ps_trq = None
    q2ld = [place(f"q2ld{j}", [128, I], BF16, bA + 49152 + 8192 * j) for j in range(2)]
    outt = place("outt", [128, D], F32, bA + 32768)

    ps_a = nc.alloc_psum_tensor("ps_a", [128, 512], F32)
    ps_b = nc.alloc_psum_tensor("ps_b", [128, 512], F32)
    ps_A = [nc.alloc_psum_tensor(f"ps_A{j}", [128, 128], F32) for j in range(4)]
    ps_y = nc.alloc_psum_tensor("ps_y", [128, 128], F32)
    ps_d = nc.alloc_psum_tensor("ps_d", [128, 64], F32)
    nc.psum_base = nc.lookup_mloc(ps_A[0]).bank
    ps_t = [nc.alloc_psum_tensor(f"ps_t{j}", [128, 128], BF16) for j in range(6)]
    nc.psum_base = 8

    pred = {}

    def schedule(c: Ctx):
        V, S, Tn, G, L, O, P, Fo, X = "V", "S", "T", "G", "L", "O", "P", "F", "X"

        def vchain(builds, eng="vector"):
            sem = {"vector": V, "gpsimd": G, "scalar": S}[eng]
            for bld in builds:
                c.on(eng, bld, incs=[(sem, 1)])
                c.wait(eng, [(sem, c.val(sem))])

        def ld3(dst, src):
            c.dma("sync", lambda e, d=dst, s_=src: e.dma_start(
                out=d[:].rearrange("p (k o) -> p k o", k=KD),
                in_=s_.rearrange("(k p) o -> p k o", p=128)), L)
        ld3(wk, wkT_d); ld3(wv, wvT_d); ld3(wr, wrT_d); ld3(wo, woT_d)
        for dst, src in [(idf, idf_d), (identb, idb_d), (mdiag, mdiag_d),
                         (m16, m16_d), (m32, m32_d), (m64, m64_d),
                         (lam, lam_d), (tfb, tf_d)]:
            c.dma("sync", lambda e, d=dst, s_=src: e.dma_start(out=d[:], in_=s_), L)
        L_w = c.val(L)

        def tile_wd(kt):
            return wdT_d[512 * kt:512 * (kt + 1), :].rearrange("(a p) d -> p a d", p=128)

        for j in range(24):
            m, kt = j // 8, j % 8
            tile = ((wgT_d, wuT_d)[m][128 * kt:128 * (kt + 1), :] if m < 2 else tile_wd(kt))
            c.dma("sync", lambda e, t=tile, b=j % 2, m=m: e.dma_start(
                    out=(stag[b][:] if m < 2 else stag[b][:].rearrange("p (a d) -> p a d", a=4)),
                    in_=t),
                 L, waits=([(V, (j - 2) + 1)] if j >= 2 else []))
            c.on("vector", lambda e, b=j % 2, jj=j: e.reduce_sum(
                    acc[:, jj:jj + 1], stag[b][:], axis=AX, apply_absolute_value=True),
                 waits=[(L, c.val(L))], incs=[(V, 1)])
        for m in range(3):
            c.on("vector", lambda e, m=m: e.reduce_sum(
                red1[:, m:m + 1], acc[:, 8 * m:8 * (m + 1)], axis=AX),
                incs=([(V, 1)] if m == 2 else []))
        c.on("vector", lambda e: e.memset(ones_col[:], 1.0))
        c.on("vector", lambda e: e.memset(ones_row[:], 1.0))
        c.on("vector", lambda e: e.memset(ones128[:], 1.0))
        c.on("vector", lambda e: e.memset(S_all[:], 0.0))
        c.on("vector", lambda e: e.memset(epsc[:], 1e-5), incs=[(V, 1)])
        c.on("tensor", lambda e: e.matmul(ps_a[0:1, 0:3], ones_col[:], red1[:, 0:3],
                                          start=True, stop=True),
             waits=[(V, c.val(V))], incs=[(Tn, 1)])
        c.wait("vector", [(Tn, c.val(Tn))])
        vchain([
            lambda e: e.tensor_scalar(mrow[0:1, 6:9], ps_a[0:1, 0:3],
                                      1.0 / (D * I), 1e-5, OP.mult, OP.max),
            lambda e: e.tensor_scalar_mul(mrow[0:1, 0:3], mrow[0:1, 6:9], 0.5),
            lambda e: e.tensor_scalar_mul(mrow[0:1, 3:6], mrow[0:1, 6:9], -0.5),
        ])
        c.on("tensor", lambda e: e.matmul(ps_b[:, 0:9], ones_row[:], mrow[:],
                                          start=True, stop=True),
             waits=[(V, c.val(V))], incs=[(Tn, 1)])
        c.on("vector", lambda e: e.tensor_copy(thr[:], ps_b[:, 0:9]),
             waits=[(Tn, c.val(Tn))], incs=[(V, 1)])
        V_thr = c.val(V)

        c.wait("vector", [(L, L_w)])
        vchain([lambda e: e.reciprocal(mu[:], lam[:])])
        for m in range(KD):
            sl = slice(128 * m, 128 * (m + 1))
            s16 = slice(16 * m, 16 * (m + 1))
            vchain([
                lambda e, m=m, sl=sl: e.tensor_scalar_mul(fF[:, sl], ones128[:], lam[:, m:m + 1]),
                lambda e, m=m, sl=sl: e.tensor_tensor_scan(
                    tRS[:, sl], fF[:, sl], fF[:, sl], 1.0, OP.mult, OP.bypass),
                lambda e, m=m: e.tensor_scalar_mul(fF[:, 128 * m:128 * m + 16], ones128[:, 0:16], mu[:, m:m + 1]),
                lambda e, m=m, s16=s16: e.tensor_tensor_scan(
                    tMU[:, s16], fF[:, 128 * m:128 * m + 16], fF[:, 128 * m:128 * m + 16],
                    1.0, OP.mult, OP.bypass),
                lambda e, m=m: e.memset(fF[:, 128 * m:128 * m + 1], 1.0),
                lambda e, m=m, s16=s16: e.tensor_copy(fF[:, 128 * m + 1:128 * m + 16], tMU[:, 16 * m:16 * m + 15]),
                lambda e, m=m: e.tensor_scalar_mul(fF[:, 128 * m:128 * m + 16],
                                                   fF[:, 128 * m:128 * m + 16], tfb[:, m:m + 1]),
                lambda e, m=m: e.tensor_copy(
                    tK16d[:, 128 * m:128 * (m + 1)].rearrange("p (r cc) -> p r cc", r=8),
                    fF[:, 128 * m:128 * m + 16].unsqueeze(1).broadcast_to([128, 8, 16])),
                lambda e, m=m: e.memset(fF[:, 128 * m + 64:128 * m + 65], 1.0),
                lambda e, m=m: e.tensor_copy(fF[:, 128 * m + 65:128 * (m + 1)],
                                             tRS[:, 128 * m:128 * m + 63]),
                lambda e, m=m: e.tensor_copy(
                    tR16[:, 128 * m:128 * (m + 1)].rearrange("p (r cc) -> p r cc", r=8),
                    fF[:, 128 * m + 64:128 * m + 80].unsqueeze(1).broadcast_to([128, 8, 16])),
                lambda e, m=m: e.tensor_copy(
                    tR32[:, 128 * m:128 * (m + 1)].rearrange("p (r cc) -> p r cc", r=4),
                    fF[:, 128 * m + 64:128 * m + 96].unsqueeze(1).broadcast_to([128, 4, 32])),
                lambda e, m=m: e.tensor_copy(
                    tR64[:, 128 * m:128 * (m + 1)].rearrange("p (r cc) -> p r cc", r=2),
                    fF[:, 128 * m + 64:128 * (m + 1)].unsqueeze(1).broadcast_to([128, 2, 64])),
                lambda e, m=m, sl=sl: e.tensor_scalar_mul(tK16[:, sl], tK16d[:, sl],
                                                          tRS[:, 128 * m + 15:128 * m + 16]),
                lambda e, m=m: e.tensor_scalar_mul(fF[:, 128 * m:128 * m + 16],
                                                   ones128[:, 0:16], tRS[:, 128 * m + 15:128 * m + 16]),
                lambda e, m=m: e.memset(fF[:, 128 * m + 16:128 * m + 32], 1.0),
                lambda e, m=m, sl=sl: e.scalar_tensor_tensor(
                    tK32[:, sl].rearrange("p (r cc) -> p r cc", r=4),
                    tK16[:, sl].rearrange("p (r cc) -> p r cc", r=4), 1.0,
                    fF[:, 128 * m:128 * m + 32].unsqueeze(1).broadcast_to([128, 4, 32]),
                    OP.mult, OP.mult),
                lambda e, m=m: e.tensor_scalar_mul(fF[:, 128 * m:128 * m + 32],
                                                   ones128[:, 0:32], tRS[:, 128 * m + 31:128 * m + 32]),
                lambda e, m=m: e.memset(fF[:, 128 * m + 32:128 * m + 64], 1.0),
                lambda e, m=m, sl=sl: e.scalar_tensor_tensor(
                    tK64[:, sl].rearrange("p (r cc) -> p r cc", r=2),
                    tK32[:, sl].rearrange("p (r cc) -> p r cc", r=2), 1.0,
                    fF[:, 128 * m:128 * m + 64].unsqueeze(1).broadcast_to([128, 2, 64]),
                    OP.mult, OP.mult),
                lambda e, m=m: e.tensor_scalar_mul(fF[:, 128 * m:128 * m + 64],
                                                   ones128[:, 0:64], tRS[:, 128 * m + 62:128 * m + 63]),
                lambda e, m=m: e.tensor_scalar_mul(fF[:, 128 * m + 64:128 * (m + 1)],
                                                   ones128[:, 0:64], mu[:, m:m + 1]),
                lambda e, m=m, sl=sl: e.tensor_mul(tKS[:, sl], tK64[:, sl], fF[:, sl]),
            ])
        V_tbl = c.val(V)

        v_x2_done = {-1: 0}
        v_proj_done = {-1: 0, -2: 0}
        tn_lnB = {}
        x_fmt_b = [x_fmt0, x_fmt1]
        y_pair_ev = {}
        s_ypair = {}
        tn_yintra = {}
        v_lvl_prev = 0
        s_trp_prev = 0
        for ci in range(NCH):
            x_fmt = x_fmt_b[ci % 2]
            w_ld = [(V, max(V_tbl, v_proj_done[ci - 2]))]
            c.dma("sync", lambda e, ci=ci, x_fmt=x_fmt: e.dma_start(
                out=x_fmt[:].rearrange("p (m t) -> p m t", m=KD),
                in_=x_fm_d[:, C * ci:C * (ci + 1)].rearrange("(m p) t -> p m t", p=128)),
                L, waits=w_ld)
            Lxf = c.val(L)
            c.dma("sync", lambda e, ci=ci: e.dma_start(
                out=x_tmt[:], in_=x_tm_d[C * ci:C * (ci + 1), :]), L,
                waits=[(V, v_x2_done[ci - 1])])
            Lx = c.val(L)

            w_sq = ([(Tn, tn_lnB[ci - 1])] if ci >= 1 else [])
            c.on("scalar", lambda e, x_fmt=x_fmt: e.activation(sq_fm[:], x_fmt[:], AF.Square),
                 waits=[(L, Lxf)] + w_sq, incs=[(S, 1)])
            c.wait("tensor", [(L, Lxf), (S, c.val(S)), (V, max(V_tbl, v_x2_done[ci - 1]))])
            for m in range(KD):
                c.on("tensor", lambda e, m=m, x_fmt=x_fmt: e.matmul(
                    ps_a[0:1, 0:128], ones_col[:], x_fmt[:, 128 * m:128 * (m + 1)],
                    start=(m == 0), stop=(m == KD - 1)))
            for m in range(KD):
                c.on("tensor", lambda e, m=m: e.matmul(
                    ps_b[0:1, 0:128], ones_col[:], sq_fm[:, 128 * m:128 * (m + 1)],
                    start=(m == 0), stop=(m == KD - 1)),
                    incs=([(Tn, 1)] if m == KD - 1 else []))
            tn_lnB[ci] = c.val(Tn)
            c.wait("vector", [(Tn, c.val(Tn))])
            vchain([
                lambda e: e.tensor_scalar_mul(srow[0:1, 0:128], ps_a[0:1, 0:128], 1.0 / D),
                lambda e: e.tensor_scalar_mul(tmp1[0:1, :], ps_b[0:1, 0:128], 1.0 / D),
                lambda e: e.tensor_mul(srow[0:1, 128:256], srow[0:1, 0:128], srow[0:1, 0:128]),
                lambda e: e.tensor_sub(tmp1[0:1, :], tmp1[0:1, :], srow[0:1, 128:256]),
            ])
            c.on("scalar", lambda e: e.activation(tmp1[0:1, :], tmp1[0:1, :], AF.Sqrt, bias=epsc[0:1, 0:1]),
                 waits=[(V, c.val(V))], incs=[(S, 1)])
            c.wait("vector", [(S, c.val(S))])
            vchain([lambda e: e.reciprocal(srow[0:1, 128:256], tmp1[0:1, :])])
            c.on("tensor", lambda e: e.matmul(ps_a[:, 0:256], ones_row[:], srow[:],
                                              start=True, stop=True),
                 waits=[(V, c.val(V))], incs=[(Tn, 1)])
            c.wait("vector", [(Tn, c.val(Tn))])
            vchain([
                lambda e, x_fmt=x_fmt: e.scalar_tensor_tensor(
                    x_fmt[:].rearrange("p (m t) -> p m t", m=KD),
                    x_fmt[:].rearrange("p (m t) -> p m t", m=KD), 1.0,
                    ps_a[:, 0:128].unsqueeze(1).broadcast_to([128, KD, 128]),
                    OP.mult, OP.subtract),
                lambda e, x_fmt=x_fmt: e.tensor_tensor(
                    x_fmt[:].rearrange("p (m t) -> p m t", m=KD),
                    x_fmt[:].rearrange("p (m t) -> p m t", m=KD),
                    ps_a[:, 128:256].unsqueeze(1).broadcast_to([128, KD, 128]),
                    OP.mult),
            ])
            V_xn = c.val(V)

            ev_done = {}
            for mi, (wmat, dst) in enumerate([(wk, k_fm), (wv, v_fm), (wr, r_fm)]):
                for m in range(KD):
                    gidx = mi * KD + m
                    bank = (ps_a, ps_b)[gidx % 2]
                    if gidx >= 2:
                        c.wait("tensor", [ev_done[gidx - 2]])
                    else:
                        c.wait("tensor", [(V, V_xn)])
                    for kk in range(KD):
                        c.on("tensor", lambda e, m=m, kk=kk, w=wmat, bank=bank, x_fmt=x_fmt: e.matmul(
                            bank[:, 0:128], w[:, kk * D + 128 * m:kk * D + 128 * (m + 1)],
                            x_fmt[:, 128 * kk:128 * (kk + 1)],
                            start=(kk == 0), stop=(kk == KD - 1)),
                            incs=([(Tn, 1)] if kk == KD - 1 else []))
                    if mi < 2:
                        w_kx = [(X, 16 * ci)] if (gidx == 0 and ci >= 1) else []
                        c.on("vector", lambda e, m=m, d=dst, bank=bank: e.tensor_copy(
                            d[:, 128 * m:128 * (m + 1)], bank[:, 0:128]),
                            waits=[(Tn, c.val(Tn))] + w_kx, incs=[(V, 1)])
                        ev_done[gidx] = (V, c.val(V))
                    else:
                        c.on("scalar", lambda e, m=m, bank=bank: e.activation(
                            r_fm[:, 128 * m:128 * (m + 1)], bank[:, 0:128], AF.Sigmoid),
                            waits=[(Tn, c.val(Tn))], incs=[(S, 1)])
                        ev_done[gidx] = (S, c.val(S))
            S_r = c.val(S)
            v_proj_done[ci] = c.val(V)

            var8_done = {}
            varRS_done = {}

            def emit_var8(pi, w_extra):
                csl = slice(128 * pi, 128 * (pi + 1))
                c.wait("vector", [(S, S_r), (V, v_proj_done[ci])] + w_extra)
                for li, (dst, src, tab) in enumerate([
                        (0, r_fm, tR16), (1, r_fm, tR32), (2, r_fm, tR64),
                        (4, k_fm, tK16d), (5, k_fm, tK16), (6, k_fm, tK32),
                        (7, k_fm, tK64), (8, k_fm, tKS)]):
                    c.on("vector", lambda e, dst=dst, s_=src, t_=tab, csl=csl: e.tensor_mul(
                        pvar[:, 128 * dst:128 * (dst + 1)], s_[:, csl], t_[:, csl]),
                        incs=([(V, 1)] if li == 7 else []))
                var8_done[pi] = c.val(V)

            def emit_varRS(pi, w_extra):
                csl = slice(128 * pi, 128 * (pi + 1))
                c.wait("vector", w_extra)
                c.on("vector", lambda e, csl=csl: e.tensor_mul(
                    pvar[:, 3 * 128:4 * 128], r_fm[:, csl], tRS[:, csl]),
                    incs=[(V, 1)])
                varRS_done[pi] = c.val(V)

            w0 = ([(Tn, tn_yintra[(ci - 1, H - 1)])]
                  if (ci - 1, H - 1) in tn_yintra else [])
            emit_var8(0, w0)
            emit_varRS(0, w0)
            v_merge_h1 = 0
            for pi in range(NP):
                csl = slice(128 * pi, 128 * (pi + 1))
                c.wait("tensor", [(V, var8_done[pi]), (S, s_trp_prev)])
                c.on("tensor", lambda e, csl=csl: e.transpose(ps_A[0][:], v_fm[:, csl], idf[:]))
                c.on("tensor", lambda e: e.transpose(ps_A[1][:], pvar[:, 8 * 128:9 * 128], idf[:]),
                     incs=[(Tn, 1)])
                c.on("scalar", lambda e: e.activation(v_tm[:], ps_A[0][:], AF.Copy),
                     waits=[(Tn, c.val(Tn))])
                c.on("scalar", lambda e: e.activation(ks_tm[:], ps_A[1][:], AF.Copy),
                     incs=[(S, 1)])
                s_trp_prev = c.val(S)
                perm0 = (2, 3, 0, 1)
                for li, (kvi, rvi) in enumerate([(4, 0), (5, 0), (6, 1), (7, 2)]):
                    if li == 2:
                        c.wait("tensor", [(S, s_trp_prev)])
                    c.on("tensor", lambda e, li=li, kvi=kvi, rvi=rvi: e.matmul(
                        ps_A[perm0[li]][:], pvar[0:64, 128 * kvi:128 * (kvi + 1)],
                        pvar[0:64, 128 * rvi:128 * (rvi + 1)],
                        start=True, stop=True),
                        incs=([(Tn, 1)] if li == 3 else []))
                tnA0 = c.val(Tn)
                h0 = 2 * pi
                w_g = [(Tn, tnA0)]
                key = (ci, h0 - 2) if h0 >= 2 else (ci - 1, H - 2)
                if key in tn_yintra:
                    w_g.append((Tn, tn_yintra[key]))
                c.wait("vector", w_g)
                asb0 = A_sb[0]
                c.on("vector", lambda e: e.tensor_mul(asb0[:], ps_A[2][:], mdiag[:]))
                c.on("vector", lambda e: e.copy_predicated(asb0[:], m16[:], ps_A[3][:]))
                c.on("vector", lambda e: e.copy_predicated(asb0[:], m32[:], ps_A[0][:]))
                c.on("vector", lambda e: e.copy_predicated(asb0[:], m64[:], ps_A[1][:]),
                     incs=[(V, 1)])
                c.wait("vector", [(V, c.val(V))])
                v_merge_h0 = c.val(V)
                h1banks = (ps_a, ps_b, ps_A[2], ps_A[3])
                for li, (kvi, rvi) in enumerate([(4, 0), (5, 0), (6, 1), (7, 2)]):
                    if li == 0:
                        c.wait("tensor", [(V, v_merge_h1)])
                    if li == 2:
                        c.wait("tensor", [(V, v_merge_h0)])
                    c.on("tensor", lambda e, li=li, kvi=kvi, rvi=rvi: e.matmul(
                        h1banks[li][:, 0:128], pvar[64:128, 128 * kvi:128 * (kvi + 1)],
                        pvar[64:128, 128 * rvi:128 * (rvi + 1)],
                        start=True, stop=True),
                        incs=([(Tn, 1)] if li == 3 else []))
                tnA1 = c.val(Tn)
                h1 = 2 * pi + 1
                w_g = [(Tn, tnA1)]
                key = (ci, h1 - 2) if h1 >= 2 else (ci - 1, H - 1)
                if key in tn_yintra:
                    w_g.append((Tn, tn_yintra[key]))
                c.wait("vector", w_g)
                asb1 = A_sb[1]
                c.on("vector", lambda e: e.tensor_mul(asb1[:], ps_a[:, 0:128], mdiag[:]))
                c.on("vector", lambda e: e.copy_predicated(asb1[:], m16[:], ps_b[:, 0:128]))
                c.on("vector", lambda e: e.copy_predicated(asb1[:], m32[:], ps_A[2][:]))
                c.on("vector", lambda e: e.copy_predicated(asb1[:], m64[:], ps_A[3][:]),
                     incs=[(V, 1)])
                c.wait("vector", [(V, c.val(V))])
                v_merge_h1 = c.val(V)
                if pi + 1 < NP:
                    emit_var8(pi + 1, [(Tn, tnA1)])
                for hh in range(2):
                    lo = hh * 64
                    prng = slice(lo, lo + 64)
                    asb = A_sb[hh]
                    if hh == 0:
                        prev_s = s_ypair.get((ci, pi - 1), s_ypair.get((ci - 1, NP - 1), 0))
                        c.wait("tensor", [(S, prev_s), (V, varRS_done[pi])])
                    else:
                        c.wait("tensor", [(V, v_merge_h1)])
                    c.on("tensor", lambda e, pi=pi, lo=lo, prng=prng: e.matmul(
                        ps_y[lo:lo + 64, :], S_all[prng, 64 * pi:64 * (pi + 1)],
                        pvar[prng, 3 * 128:4 * 128],
                        start=True, stop=False,
                        tile_position=(lo, lo)))
                    c.on("tensor", lambda e, lo=lo, asb=asb: e.matmul(
                        ps_y[lo:lo + 64, :], v_tm[:, lo:lo + 64],
                        asb[:],
                        start=False, stop=True,
                        tile_position=(0, lo)),
                        incs=[(Tn, 1)])
                    tn_yintra[(ci, 2 * pi + hh)] = c.val(Tn)
                tn_yh1 = c.val(Tn)
                prev = y_pair_ev.get((ci, pi - 1), y_pair_ev.get((ci - 1, NP - 1), 0))
                c.wait("tensor", [(V, prev)])
                for hh in range(2):
                    lo = hh * 64
                    c.on("tensor", lambda e, lo=lo: e.matmul(
                        ps_d[lo:lo + 64, :], ks_tm[:, lo:lo + 64],
                        v_tm[:, lo:lo + 64],
                        start=True, stop=True,
                        tile_position=(0, lo)),
                        incs=([(Tn, 1)] if lo == 64 else []))
                if pi + 1 < NP:
                    emit_varRS(pi + 1, [(Tn, tn_yh1)])
                c.on("scalar", lambda e, pi=pi: e.activation(
                        y_fm[:, 128 * pi:128 * (pi + 1)], ps_y[:], AF.Copy),
                     waits=[(Tn, tn_yh1)], incs=[(S, 1)])
                s_yev = c.val(S)
                c.wait("vector", [(Tn, c.val(Tn))])
                vchain([
                    lambda e, pi=pi: e.scalar_tensor_tensor(
                        S_all[:, 64 * pi:64 * (pi + 1)], S_all[:, 64 * pi:64 * (pi + 1)],
                        tRS[:, 128 * pi + 127:128 * (pi + 1)], ps_d[:],
                        OP.mult, OP.add),
                ])
                y_pair_ev[(ci, pi)] = c.val(V)
                s_ypair[(ci, pi)] = s_yev
            V_y = c.val(V)
            S_y = c.val(S)

            for n in range(2):
                bank = (ps_a, ps_b)[n]
                c.wait("tensor", [(V, V_y), (S, S_y)])
                for kk in range(KD):
                    c.on("tensor", lambda e, kk=kk, n=n, bank=bank: e.matmul(
                        bank[:], y_fm[:, 128 * kk:128 * (kk + 1)],
                        wo[:, kk * D + 512 * n:kk * D + 512 * (n + 1)],
                        start=(kk == 0), stop=(kk == KD - 1)),
                        incs=([(Tn, 1)] if kk == KD - 1 else []))
                w_o = ([(X, 16 * ci)] if ci >= 1 else []) + [(L, Lx)] if n == 0 else []
                c.on("vector", lambda e, n=n, bank=bank: e.tensor_tensor(
                        x2_sb[:, 512 * n:512 * (n + 1)], bank[:],
                        x_tmt[:, 512 * n:512 * (n + 1)], OP.add),
                     waits=[(Tn, c.val(Tn))] + w_o, incs=[(V, 1)])
            v_x2_done[ci] = c.val(V)
            c.dma("gpsimd", lambda e, ci=ci: e.dma_start(
                out=x2_scr[C * ci:C * (ci + 1), :], in_=x2_sb[:]),
                X, waits=[(V, c.val(V))])
        V_endA = c.val(V)
        Tn_endA = c.val(Tn)

        G_endA = c.val(G)
        for j in range(16):
            m, kt = j // 8, j % 8
            tile = (wgT_d, wuT_d)[m][128 * kt:128 * (kt + 1), :]
            w = [(V, V_endA), (G, G_endA), (S, c.val(S))] if j < 2 else [(V, w2a_v0 + 3 * (j - 1))]
            c.dma("sync", lambda e, t=tile, b=j % 2: e.dma_start(out=stag[b][:], in_=t),
                 L, waits=w)
            if j == 0:
                w2a_v0 = c.val(V)
            dsl = (wgq, wuq)[m][:, kt * I:(kt + 1) * I]
            w_first = [(Tn, Tn_endA)] if j == 0 else []
            c.on("vector", lambda e, b=j % 2, m=m: e.tensor_scalar(
                    wqa[:], stag[b][:], thr[:, m:m + 1], None, OP.is_ge),
                 waits=[(L, c.val(L))] + w_first, incs=[(V, 1)])
            c.on("vector", lambda e, b=j % 2, m=m: e.tensor_scalar(
                    wqb[:], stag[b][:], thr[:, 3 + m:4 + m], None, OP.is_le),
                 incs=[(V, 1)])
            c.on("vector", lambda e, d=dsl: e.tensor_sub(d, wqa[:], wqb[:]),
                 incs=[(V, 1)])
        V_wq = c.val(V)

        SUM, SUMSQ, MEAN, MS, M2, VAR, SD, RSTD, AMAX, C1, RC, S1, RS1, AG, AU, ZMX, C2, S2 = range(18)
        AG2, AU2 = 18, 19
        v_xn_done = {-1: V_wq, -2: V_wq}
        v_var = {}
        s_sqrt = {}
        q1done = {}
        lx_ld = {}
        tn_q1tr = {}
        v_q2c = {-1: 0}
        v_zt = {(-1, 0): 0, (-1, 1): 0}
        s_gu_done = {}
        last_s_bank = {bk: 0 for bk in range(6)}
        vtq = {}

        def emit_load(j):
            if j >= NT:
                return
            c.dma("sync", lambda e, b=j % 2, j=j: e.dma_start(
                    out=x2t[b][:], in_=x2_scr[128 * j:128 * (j + 1), :]),
                 L, waits=[(V, v_xn_done[j - 2])] + [(X, 16 * (j + 1))])
            lx_ld[j] = c.val(L)

        def emit_Apre(j):
            if j >= NT:
                return
            b = j % 2
            c.wait("vector", [(L, lx_ld[j])])
            vchain([
                lambda e, b=b: e.reduce_sum(sm[:, SUM:SUM + 1], x2t[b][:], axis=AX),
                lambda e, b=b: e.tensor_mul(sqt[:], x2t[b][:], x2t[b][:]),
                lambda e: e.reduce_sum(sm[:, SUMSQ:SUMSQ + 1], sqt[:], axis=AX),
                lambda e: e.tensor_scalar_mul(sm[:, MEAN:MEAN + 1], sm[:, SUM:SUM + 1], 1.0 / D),
                lambda e: e.tensor_scalar_mul(sm[:, MS:MS + 1], sm[:, SUMSQ:SUMSQ + 1], 1.0 / D),
                lambda e: e.tensor_mul(sm[:, M2:M2 + 1], sm[:, MEAN:MEAN + 1], sm[:, MEAN:MEAN + 1]),
                lambda e: e.tensor_sub(sm[:, VAR:VAR + 1], sm[:, MS:MS + 1], sm[:, M2:M2 + 1]),
            ])
            v_var[j] = c.val(V)

        def emit_Sqrt(j):
            if j >= NT:
                return
            c.on("scalar", lambda e: e.activation(sm[:, SD:SD + 1], sm[:, VAR:VAR + 1],
                                                  AF.Sqrt, bias=epsc[:, 0:1]),
                 waits=[(V, v_var[j])], incs=[(S, 1)])
            s_sqrt[j] = c.val(S)

        def emit_Apost(j):
            if j >= NT:
                return
            b = j % 2
            ag, au = (AG, AU) if b == 0 else (AG2, AU2)
            c.wait("vector", [(S, s_sqrt[j])])
            vchain([
                lambda e: e.reciprocal(sm[:, RSTD:RSTD + 1], sm[:, SD:SD + 1]),
                lambda e, b=b: e.tensor_scalar(
                    xnt[:], x2t[b][:], sm[:, MEAN:MEAN + 1], sm[:, RSTD:RSTD + 1],
                    OP.subtract, OP.mult),
            ])
            v_xn_done[j] = c.val(V)
            vchain([
                lambda e: e.reduce_max(sm[:, AMAX:AMAX + 1], xnt[:], axis=AX,
                                       apply_absolute_value=True),
                lambda e: e.tensor_scalar_max(sm[:, C1:C1 + 1], sm[:, AMAX:AMAX + 1], 1e-5),
                lambda e: e.reciprocal(sm[:, RC:RC + 1], sm[:, C1:C1 + 1]),
                lambda e: e.tensor_scalar_mul(sm[:, S1:S1 + 1], sm[:, RC:RC + 1], 127.0),
                lambda e: e.tensor_scalar_mul(sm[:, RS1:RS1 + 1], sm[:, C1:C1 + 1], 1.0 / 127.0),
            ])
            c.wait("vector", [(S, s_gu_done.get(j - 2, 0))])
            vchain([
                lambda e, ag=ag: e.tensor_mul(sm[:, ag:ag + 1], sm[:, RS1:RS1 + 1], thr[:, 6:7]),
                lambda e, au=au: e.tensor_mul(sm[:, au:au + 1], sm[:, RS1:RS1 + 1], thr[:, 7:8]),
                lambda e: e.tensor_scalar_mul(u1t[:], xnt[:], sm[:, S1:S1 + 1]),
                lambda e: e.tensor_copy(fl1[:], u1t[:]),
            ])
            c.wait("vector", [(Tn, (lambda jj=j - 2: pred.get(("t2", jj), 0)))])
            vchain([
                lambda e: e.scalar_tensor_tensor(t1a[:], u1t[:], -0.5, fl1[:], OP.add, OP.is_ge),
                lambda e: e.scalar_tensor_tensor(t1b[:], u1t[:], 0.5, fl1[:], OP.add, OP.is_lt),
                lambda e: e.tensor_add(u1t[:], fl1[:], t1a[:]),
            ])
            c.wait("vector", [(Tn, tn_q1tr.get(j - 1, 0))])
            vchain([
                lambda e: e.tensor_sub(q1tm[:], u1t[:], t1b[:]),
            ])
            q1done[j] = c.val(V)

        def emit_T2(j):
            for hf in range(2):
                qb = (q2tm, q2tm2)[hf]
                for jj in range(16):
                    bk = jj % 6
                    c.on("tensor", lambda e, jj=jj, bk=bk, qb=qb: e.transpose(
                            ps_t[bk][:], qb[:, 128 * jj:128 * (jj + 1)], identb[:]),
                         waits=[(V, vtq[(j, hf)]), (S, last_s_bank[bk])], incs=[(Tn, 1)])
                    bb = jj // 8
                    nsp = 4 * j + 2 * hf + bb
                    w_sp = [(P, 16 * (nsp - 1))] if (jj % 8 == 0 and nsp >= 2) else []
                    c.on("scalar", lambda e, jj=jj, bb=bb, bk=bk: e.activation(
                            q2pp[bb][:, 128 * (jj % 8):128 * (jj % 8 + 1)], ps_t[bk][:], AF.Copy),
                         waits=[(Tn, c.val(Tn))] + w_sp, incs=[(S, 1)])
                    last_s_bank[bk] = c.val(S)
                    if jj % 8 == 7:
                        c.dma("gpsimd", lambda e, j=j, hf=hf, bb=bb: e.dma_start(
                                out=q2_scr[j][:, (2 * hf + bb) * 1024:(2 * hf + bb + 1) * 1024],
                                in_=q2pp[bb][:]),
                             P, waits=[(S, c.val(S))])
            pred[("t2", j)] = c.val(Tn)

        emit_load(0)
        emit_load(1)
        emit_Apre(0)
        emit_Sqrt(0)
        emit_Apost(0)
        emit_Apre(1)
        for i in range(NT):
            b = i % 2
            ag, au = (AG, AU) if b == 0 else (AG2, AU2)
            emit_load(i + 2)
            for k in range(KD):
                bk = k % 6
                c.on("tensor", lambda e, k=k, bk=bk: e.transpose(
                        ps_t[bk][:], q1tm[:, 128 * k:128 * (k + 1)], identb[:]),
                     waits=[(V, q1done[i]), (S, last_s_bank[bk])], incs=[(Tn, 1)])
                c.on("scalar", lambda e, k=k, bk=bk: e.activation(
                        q1fm[:, 128 * k:128 * (k + 1)], ps_t[bk][:], AF.Copy),
                     waits=[(Tn, c.val(Tn))], incs=[(S, 1)])
                last_s_bank[bk] = c.val(S)
            tn_q1tr[i] = c.val(Tn)
            S_q1 = c.val(S)
            emit_Sqrt(i + 1)
            emit_Apost(i + 1)
            s_evac = {}
            grp = 0
            for hhf in range(2):
                for part in range(2):
                    wsrc = (wgq, wuq)[part]
                    alpha_col = (ag, au)[part]
                    func = (AF.Silu, AF.Copy)[part]
                    dst_sb = (gt, ut)[part]
                    for ns in range(4):
                        if grp == 11 and i >= 1:
                            emit_T2(i - 1)
                        bank = (ps_a, ps_b)[grp % 2]
                        w_bank = [(S, s_evac[grp - 2])] if grp >= 2 else []
                        c.wait("tensor", [(S, S_q1)] + w_bank)
                        for k in range(KD):
                            c.on("tensor", lambda e, k=k, hf=hhf, ns=ns, w=wsrc, bank=bank:
                                 e.matmul(bank[:],
                                          q1fm[:, 128 * k:128 * (k + 1)],
                                          w[:, k * I + hf * HALF + 512 * ns: k * I + hf * HALF + 512 * (ns + 1)],
                                          start=(k == 0), stop=(k == KD - 1)),
                                 incs=([(Tn, 1)] if k == KD - 1 else []))
                        c.on("scalar", lambda e, bank=bank, ns=ns, ac=alpha_col, fn=func, d=dst_sb:
                             e.activation(d[:, 512 * ns:512 * (ns + 1)], bank[:], fn,
                                          scale=sm[:, ac:ac + 1]),
                             waits=[(Tn, c.val(Tn)), (G, c.val(G))], incs=[(S, 1)])
                        s_evac[grp] = c.val(S)
                        if part == 1:
                            w_z = [(V, v_zt[(i - 1, hhf)])] if ns == 0 else []
                            c.on("gpsimd", lambda e, hf=hhf, ns=ns: e.tensor_mul(
                                    zt[:, hf * HALF + 512 * ns:hf * HALF + 512 * (ns + 1)],
                                    gt[:, 512 * ns:512 * (ns + 1)], ut[:, 512 * ns:512 * (ns + 1)]),
                                 waits=[(S, c.val(S))] + w_z, incs=[(G, 1)])
                        grp += 1
            s_gu_done[i] = c.val(S)
            emit_Apre(i + 2)
            c.wait("vector", [(G, c.val(G))])
            vchain([
                lambda e: e.reduce_max(zm2[:, 0:1], zt[:, 0:HALF],
                                       axis=AX, apply_absolute_value=True),
                lambda e: e.reduce_max(zm2[:, 1:2], zt[:, HALF:I],
                                       axis=AX, apply_absolute_value=True),
                lambda e: e.tensor_max(sm[:, ZMX:ZMX + 1], zm2[:, 0:1], zm2[:, 1:2]),
                lambda e: e.tensor_scalar_max(sm[:, C2:C2 + 1], sm[:, ZMX:ZMX + 1], 1e-5),
                lambda e: e.reciprocal(sm[:, RC:RC + 1], sm[:, C2:C2 + 1]),
                lambda e: e.tensor_scalar_mul(sm[:, S2:S2 + 1], sm[:, RC:RC + 1], 127.0),
                lambda e, i=i: e.tensor_scalar_mul(s2col[:, i:i + 1], sm[:, C2:C2 + 1], 1.0 / 127.0),
            ])
            for hf in range(2):
                qb = (q2tm, q2tm2)[hf]
                vchain([
                    lambda e, hf=hf: e.tensor_scalar_mul(u2t[:], zt[:, hf * HALF:(hf + 1) * HALF],
                                                         sm[:, S2:S2 + 1]),
                ])
                v_zt[(i, hf)] = c.val(V)
                vchain([
                    lambda e: e.tensor_copy(fl2[:], u2t[:]),
                    lambda e: e.scalar_tensor_tensor(t2a[:], u2t[:], -0.5, fl2[:], OP.add, OP.is_ge),
                ])
                c.wait("vector", [(Tn, (lambda ii=i: pred.get(("t2", ii - 1), 0)))])
                vchain([
                    lambda e, qb=qb: e.tensor_add(qb[:], fl2[:], t2a[:]),
                    lambda e: e.scalar_tensor_tensor(t2a[:], u2t[:], 0.5, fl2[:], OP.add, OP.is_lt),
                    lambda e, qb=qb: e.tensor_sub(qb[:], qb[:], t2a[:]),
                ])
                vtq[(i, hf)] = c.val(V)
            v_q2c[i] = c.val(V)
        emit_T2(NT - 1)
        V_endB = c.val(V)
        Tn_endB = c.val(Tn)
        G_endB = c.val(G)

        for j in range(8):
            kt = j
            tile = tile_wd(kt)
            w = [(V, V_endB)] if j == 0 else ([(V, w2b_v0 + 3 * (j - 1))] if j >= 2
                 else [(P, 16 * 4 * NT)])
            c.dma("sync", lambda e, t=tile, b=j % 2: e.dma_start(
                    out=stag[b][:].rearrange("p (a d) -> p a d", a=4), in_=t),
                 L, waits=w)
            if j == 0:
                w2b_v0 = c.val(V)
            dsl = wdq[:, (4 * kt) * D:(4 * kt + 4) * D]
            w_first = [(Tn, Tn_endB), (G, G_endB), (S, c.val(S))] if j == 0 else []
            c.on("vector", lambda e, b=j % 2: e.tensor_scalar(
                    wqa[:], stag[b][:], thr[:, 2:3], None, OP.is_ge),
                 waits=[(L, c.val(L))] + w_first, incs=[(V, 1)])
            c.on("vector", lambda e, b=j % 2: e.tensor_scalar(
                    wqb[:], stag[b][:], thr[:, 5:6], None, OP.is_le),
                 incs=[(V, 1)])
            c.on("vector", lambda e, d=dsl: e.tensor_sub(d, wqa[:], wqb[:]),
                 incs=[(V, 1)])
        VC = c.val(V)

        vc_evac = {}
        xbase = c.val(X)
        for i in range(NT):
            b = i % 2
            w_ld = [(V, vc_evac[i - 2])] if i >= 2 else [(V, VC)]
            c.dma("sync", lambda e, b=b, i=i: e.dma_start(out=q2ld[b][:], in_=q2_scr[i]),
                 L, waits=w_ld + [(P, 64 * (i + 1))])
            c.dma("sync", lambda e, b=b, i=i: e.dma_start(
                    out=x2t[b][:], in_=x2_scr[128 * i:128 * (i + 1), :]), L)
            Lx = c.val(L)
            vchain([
                lambda e, i=i: e.tensor_mul(sm[:, AG:AG + 1], s2col[:, i:i + 1], thr[:, 8:9]),
            ])
            for n in range(2):
                bank = (ps_a, ps_b)[n]
                c.wait("tensor", [(L, Lx), (V, c.val(V))])
                for seg in range(KI):
                    c.on("tensor", lambda e, seg=seg, n=n, b=b, bank=bank: e.matmul(
                        bank[:], q2ld[b][:, 128 * seg:128 * (seg + 1)],
                        wdq[:, seg * D + 512 * n: seg * D + 512 * (n + 1)],
                        start=(seg == 0), stop=(seg == KI - 1)),
                         incs=([(Tn, 1)] if seg == KI - 1 else []))
                w_out = [(X, xbase + 16 * i)] if (n == 0 and i >= 1) else []
                c.on("vector", lambda e, n=n, b=b, bank=bank: e.scalar_tensor_tensor(
                        outt[:, 512 * n:512 * (n + 1)], bank[:], sm[:, AG:AG + 1],
                        x2t[b][:, 512 * n:512 * (n + 1)], OP.mult, OP.add),
                     waits=[(Tn, c.val(Tn))] + w_out, incs=[(V, 1)])
            vc_evac[i] = c.val(V)
            c.dma("gpsimd", lambda e, i=i: e.dma_start(
                out=out_d[128 * i:128 * (i + 1), :], in_=outt[:]),
                X, waits=[(V, c.val(V))])
        c.wait("sync", [(X, c.val(X)), (P, c.val(P)), (L, c.val(L))])

    sems = {name: nc.alloc_semaphore(name)
            for name in ["V", "S", "T", "G", "L", "O", "P", "F", "X"]}
    with nc.Block() as block:
        for name in ENGS:
            def make(n):
                def g(eng):
                    schedule(Ctx(sems, n, eng))
                return g
            getattr(block, name)(make(name))
    return nc


def kernel(L0_attn_x, L0_attn_Wk, L0_attn_Wv, L0_attn_Wr, L0_attn_Wo,
           L0_attn_time_decay, L0_attn_time_first,
           L0_ffn_w_gate, L0_ffn_w_up, L0_ffn_w_down):
    x = np.ascontiguousarray(np.asarray(L0_attn_x, dtype=np.float32))
    wkT = np.ascontiguousarray(np.asarray(L0_attn_Wk, np.float32).T)
    wvT = np.ascontiguousarray(np.asarray(L0_attn_Wv, np.float32).T)
    wrT = np.ascontiguousarray(np.asarray(L0_attn_Wr, np.float32).T)
    woT = np.ascontiguousarray(np.asarray(L0_attn_Wo, np.float32).T)
    wgT = np.ascontiguousarray(np.asarray(L0_ffn_w_gate, np.float32).T)
    wuT = np.ascontiguousarray(np.asarray(L0_ffn_w_up, np.float32).T)
    wdT = np.ascontiguousarray(np.asarray(L0_ffn_w_down, np.float32).T)
    lam_pc = np.ascontiguousarray(
        np.asarray(L0_attn_time_decay, np.float32).reshape(-1).reshape(KD, 128).T)
    tf_pc = np.ascontiguousarray(
        np.asarray(L0_attn_time_first, np.float32).reshape(-1).reshape(KD, 128).T)

    jj = np.arange(128)
    s_, t_ = np.meshgrid(jj, jj, indexing="ij")
    mdiag = ((s_ // 16 == t_ // 16) & (s_ <= t_)).astype(np.float32)
    m16 = (t_ // 16 - s_ // 16 == 1).astype(np.uint8)
    m32 = ((t_ // 32 - s_ // 32 == 1) & (t_ // 16 - s_ // 16 >= 2)).astype(np.uint8)
    m64 = ((t_ // 64 - s_ // 64 == 1) & (t_ // 32 - s_ // 32 >= 2)).astype(np.uint8)
    idf = np.eye(128, dtype=np.float32)
    idb = np.eye(128, dtype=ml_dtypes.bfloat16)

    nc = build_nc()
    in_maps = []
    for core in range(8):
        xi = np.ascontiguousarray(x[core])
        in_maps.append({
            "x_tm": xi,
            "x_fm": np.ascontiguousarray(xi.T),
            "wkT": wkT, "wvT": wvT, "wrT": wrT, "woT": woT,
            "wgT": wgT, "wuT": wuT, "wdT": wdT,
            "lam": lam_pc, "tf": tf_pc,
            "idf": idf, "idb": idb,
            "mdiag": mdiag, "m16": m16, "m32": m32, "m64": m64,
        })
    res = run_bass_kernel_spmd(nc, in_maps, core_ids=list(range(8)))
    out = np.stack([res.results[i]["out"] for i in range(8)], axis=0)
    return out


# version 17
# speedup vs baseline: 1.0019x; 1.0019x over previous
import numpy as np
import ml_dtypes
import concourse.bass as bass
import concourse.mybir as mybir
from concourse.bass_utils import run_bass_kernel_spmd

F32 = mybir.dt.float32
I32 = mybir.dt.int32
BF16 = mybir.dt.bfloat16
I16 = mybir.dt.int16
U8 = mybir.dt.uint8
AX = mybir.AxisListType.X
OP = mybir.AluOpType
AF = mybir.ActivationFunctionType

T, D, H, HD = 2048, 1024, 16, 64
I = 4096
C = 128
NCH = T // C
NT = T // 128
KD = D // 128
KI = I // 128
NP = H // 2
HALF = I // 2

ENGS = ["sync", "gpsimd", "scalar", "tensor", "vector"]


class Ctx:

    def __init__(self, sems, eng_name, eng):
        self.sems = sems
        self.eng_name = eng_name
        self.eng = eng
        self.cnt = {k: 0 for k in sems}

    def val(self, sem):
        return self.cnt[sem]

    def on(self, e, build, waits=(), incs=()):
        if e == self.eng_name:
            for sem, v in waits:
                if callable(v):
                    v = v()
                if v > 0:
                    self.eng.wait_ge(self.sems[sem], v)
            inst = build(self.eng)
            for sem, amt in incs:
                inst = inst.then_inc(self.sems[sem], amt)
        for sem, amt in incs:
            self.cnt[sem] += amt

    def dma(self, e, build, sem, waits=()):
        self.on(e, build, waits=list(waits) + [(sem, self.cnt[sem])], incs=[(sem, 16)])

    def wait(self, e, waits):
        if e == self.eng_name:
            for sem, v in waits:
                if callable(v):
                    v = v()
                if v > 0:
                    self.eng.wait_ge(self.sems[sem], v)


def build_nc():
    nc = bass.Bass("TRN2", target_bir_lowering=False, detect_race_conditions=False)
    x_tm_d = nc.dram_tensor("x_tm", [T, D], F32, kind="ExternalInput").ap()
    x_fm_d = nc.dram_tensor("x_fm", [D, T], F32, kind="ExternalInput").ap()
    wkT_d = nc.dram_tensor("wkT", [D, D], F32, kind="ExternalInput").ap()
    wvT_d = nc.dram_tensor("wvT", [D, D], F32, kind="ExternalInput").ap()
    wrT_d = nc.dram_tensor("wrT", [D, D], F32, kind="ExternalInput").ap()
    woT_d = nc.dram_tensor("woT", [D, D], F32, kind="ExternalInput").ap()
    wgT_d = nc.dram_tensor("wgT", [D, I], F32, kind="ExternalInput").ap()
    wuT_d = nc.dram_tensor("wuT", [D, I], F32, kind="ExternalInput").ap()
    wdT_d = nc.dram_tensor("wdT", [I, D], F32, kind="ExternalInput").ap()
    lam_d = nc.dram_tensor("lam", [128, KD], F32, kind="ExternalInput").ap()
    tf_d = nc.dram_tensor("tf", [128, KD], F32, kind="ExternalInput").ap()
    idf_d = nc.dram_tensor("idf", [128, 128], F32, kind="ExternalInput").ap()
    idb_d = nc.dram_tensor("idb", [128, 128], BF16, kind="ExternalInput").ap()
    mdiag_d = nc.dram_tensor("mdiag", [128, 128], F32, kind="ExternalInput").ap()
    m16_d = nc.dram_tensor("m16", [128, 128], U8, kind="ExternalInput").ap()
    m32_d = nc.dram_tensor("m32", [128, 128], U8, kind="ExternalInput").ap()
    m64_d = nc.dram_tensor("m64", [128, 128], U8, kind="ExternalInput").ap()
    out_d = nc.dram_tensor("out", [T, D], F32, kind="ExternalOutput").ap()
    x2_scr = nc.dram_tensor("x2scr", [T, D], F32, kind="Internal").ap()
    q2_scr = nc.dram_tensor("q2scr", [NT, 128, I], BF16, kind="Internal").ap()

    arena_h = nc.alloc_sbuf_tensor("arena", [128, 212736], U8)
    BASE = nc.lookup_mloc(arena_h).addr

    def place(name, shape, dt, off):
        return nc.alloc_sbuf_tensor_at(name, shape, dt, offset=BASE + off)

    wk = place("wk", [128, KD * D], F32, 0)
    wv = place("wv", [128, KD * D], F32, 32768)
    wr = place("wr", [128, KD * D], F32, 65536)
    wo = place("wo", [128, KD * D], F32, 98304)
    wgq = place("wgq", [128, KD * I], BF16, 0)
    wuq = place("wuq", [128, KD * I], BF16, 65536)
    wdq = place("wdq", [128, KI * D], BF16, 0)
    cb = 131072
    idf = place("idf", [128, 128], F32, cb); cb += 512
    identb = place("identb", [128, 128], BF16, cb); cb += 256
    mdiag = place("mdiag", [128, 128], F32, cb); cb += 512
    m16 = place("m16", [128, 128], U8, cb); cb += 128
    m32 = place("m32", [128, 128], U8, cb); cb += 128
    m64 = place("m64", [128, 128], U8, cb); cb += 128
    ones_col = place("ones_col", [128, 1], F32, cb); cb += 32
    ones_row = place("ones_row", [1, 128], F32, cb); cb += 512
    ones128 = place("ones128", [128, 128], F32, cb); cb += 512
    lam = place("lam", [128, KD], F32, cb); cb += 32
    tfb = place("tfb", [128, KD], F32, cb); cb += 32
    mu = place("mu", [128, KD], F32, cb); cb += 32
    tMU = place("tMU", [128, KD * 16], F32, cb); cb += 512
    srow = place("srow", [1, 256], F32, cb); cb += 1024
    epsc = place("epsc", [128, 1], F32, cb); cb += 32
    tmp1 = place("tmp1", [1, 128], F32, cb); cb += 512
    thr = place("thr", [128, 9], F32, cb); cb += 64
    acc = place("acc", [128, 24], F32, cb); cb += 96
    red1 = place("red1", [128, 3], F32, cb); cb += 32
    mrow = place("mrow", [1, 9], F32, cb); cb += 64
    sm = place("sm", [128, 24], F32, cb); cb += 96
    s2col = place("s2col", [128, NT], F32, cb); cb += 64
    zm2 = place("zm2", [128, 2], F32, cb); cb += 32
    assert cb <= 136448, cb
    AR = 136448
    TOPB = 212736
    assert AR + 76288 <= TOPB

    a = AR
    tRS = place("tRS", [128, KD * 128], F32, a); a += 4096
    tR16 = place("tR16", [128, KD * 128], F32, a); a += 4096
    tR32 = place("tR32", [128, KD * 128], F32, a); a += 4096
    tR64 = place("tR64", [128, KD * 128], F32, a); a += 4096
    tK16d = place("tK16d", [128, KD * 128], F32, a); a += 4096
    tK16 = place("tK16", [128, KD * 128], F32, a); a += 4096
    tK32 = place("tK32", [128, KD * 128], F32, a); a += 4096
    tK64 = place("tK64", [128, KD * 128], F32, a); a += 4096
    tKS = place("tKS", [128, KD * 128], F32, a); a += 4096
    fF = place("fF", [128, KD * 128], F32, a); a += 4096
    x_fmt1 = place("x_fmt1", [128, KD * 128], F32, nc.lookup_mloc(fF).addr - BASE)
    S_all = place("S_all", [128, NP * 64], F32, a); a += 2048
    x_fmt0 = place("x_fmt0", [128, KD * 128], F32, a); a += 4096
    x_tmt = place("x_tmt", [128, D], F32, a); a += 4096
    k_fm = place("k_fm", [128, KD * 128], F32, a); a += 4096
    r_fm = place("r_fm", [128, KD * 128], F32, a); a += 4096
    y_fm = place("y_fm", [128, KD * 128], F32, a)
    v_fm = place("v_fm", [128, KD * 128], F32, a); a += 4096
    x2_sb = place("x2_sb", [128, D], F32, nc.lookup_mloc(k_fm).addr - BASE)
    sq_fm = place("sq_fm", [128, KD * 128], F32, a); a += 4096
    pvar = place("pvar", [128, 9 * 128], F32, a); a += 4608
    v_tm = place("v_tm", [128, 128], F32, a); a += 512
    ks_tm = place("ks_tm", [128, 128], F32, a); a += 512
    A_sb = []
    for j in range(2):
        A_sb.append(place(f"A_sb{j}", [128, 128], F32, a)); a += 512
    assert a <= TOPB, a

    stag = [place(f"stag{j}", [128, I], F32, AR + 16384 * j) for j in range(2)]
    wqa = place("wqa", [128, I], BF16, AR + 32768)
    wqb = place("wqb", [128, I], BF16, AR + 40960)

    bA = AR
    x2t = [place(f"x2t{j}", [128, D], F32, bA + 4096 * j) for j in range(2)]
    sqt = place("sqt", [128, D], F32, bA + 8192)
    xnt = place("xnt", [128, D], F32, bA + 12288)
    u1t = place("u1t", [128, D], F32, bA + 16384)
    fl1 = place("fl1", [128, D], I32, bA + 20480)
    t1a = place("t1a", [128, D], BF16, bA + 24576)
    t1b = place("t1b", [128, D], BF16, bA + 26624)
    q1tm = place("q1tm", [128, D], BF16, bA + 28672)
    q1fm = place("q1fm", [128, D], BF16, bA + 30720)
    gt = place("gt", [128, HALF], F32, bA + 32768)
    ut = place("ut", [128, HALF], F32, bA + 40960)
    zt = place("zt", [128, I], F32, bA + 49152)
    u2t = place("u2t", [128, HALF], F32, bA + 12288)
    fl2 = place("fl2", [128, HALF], I16, bA + 8192)
    t2a = place("t2a", [128, HALF], BF16, bA + 20480)
    q2tm = place("q2tm", [128, HALF], BF16, bA + 24576)
    q2pp = [place(f"q2pp{j}", [128, 1024], BF16, bA + 65536 + 2048 * j) for j in range(2)]
    q2tm2 = place("q2tm2", [128, HALF], BF16, bA + 69632)
    ps_trq = None
    q2ld = [place(f"q2ld{j}", [128, I], BF16, bA + 49152 + 8192 * j) for j in range(3)]
    outt = place("outt", [128, D], F32, bA + 32768)

    ps_a = nc.alloc_psum_tensor("ps_a", [128, 512], F32)
    ps_b = nc.alloc_psum_tensor("ps_b", [128, 512], F32)
    ps_A = [nc.alloc_psum_tensor(f"ps_A{j}", [128, 128], F32) for j in range(4)]
    ps_y = nc.alloc_psum_tensor("ps_y", [128, 128], F32)
    ps_d = nc.alloc_psum_tensor("ps_d", [128, 64], F32)
    nc.psum_base = nc.lookup_mloc(ps_A[0]).bank
    ps_t = [nc.alloc_psum_tensor(f"ps_t{j}", [128, 128], BF16) for j in range(6)]
    nc.psum_base = 8

    pred = {}

    def schedule(c: Ctx):
        V, S, Tn, G, L, O, P, Fo, X = "V", "S", "T", "G", "L", "O", "P", "F", "X"

        def vchain(builds, eng="vector"):
            sem = {"vector": V, "gpsimd": G, "scalar": S}[eng]
            for bld in builds:
                c.on(eng, bld, incs=[(sem, 1)])
                c.wait(eng, [(sem, c.val(sem))])

        def ld3(dst, src):
            c.dma("sync", lambda e, d=dst, s_=src: e.dma_start(
                out=d[:].rearrange("p (k o) -> p k o", k=KD),
                in_=s_.rearrange("(k p) o -> p k o", p=128)), L)
        ld3(wk, wkT_d); ld3(wv, wvT_d); ld3(wr, wrT_d); ld3(wo, woT_d)
        for dst, src in [(idf, idf_d), (identb, idb_d), (mdiag, mdiag_d),
                         (m16, m16_d), (m32, m32_d), (m64, m64_d),
                         (lam, lam_d), (tfb, tf_d)]:
            c.dma("sync", lambda e, d=dst, s_=src: e.dma_start(out=d[:], in_=s_), L)
        L_w = c.val(L)

        def tile_wd(kt):
            return wdT_d[512 * kt:512 * (kt + 1), :].rearrange("(a p) d -> p a d", p=128)

        for j in range(24):
            m, kt = j // 8, j % 8
            tile = ((wgT_d, wuT_d)[m][128 * kt:128 * (kt + 1), :] if m < 2 else tile_wd(kt))
            c.dma("sync", lambda e, t=tile, b=j % 2, m=m: e.dma_start(
                    out=(stag[b][:] if m < 2 else stag[b][:].rearrange("p (a d) -> p a d", a=4)),
                    in_=t),
                 L, waits=([(V, (j - 2) + 1)] if j >= 2 else []))
            c.on("vector", lambda e, b=j % 2, jj=j: e.reduce_sum(
                    acc[:, jj:jj + 1], stag[b][:], axis=AX, apply_absolute_value=True),
                 waits=[(L, c.val(L))], incs=[(V, 1)])
        for m in range(3):
            c.on("vector", lambda e, m=m: e.reduce_sum(
                red1[:, m:m + 1], acc[:, 8 * m:8 * (m + 1)], axis=AX),
                incs=([(V, 1)] if m == 2 else []))
        c.on("vector", lambda e: e.memset(ones_col[:], 1.0))
        c.on("vector", lambda e: e.memset(ones_row[:], 1.0))
        c.on("vector", lambda e: e.memset(ones128[:], 1.0))
        c.on("vector", lambda e: e.memset(S_all[:], 0.0))
        c.on("vector", lambda e: e.memset(epsc[:], 1e-5), incs=[(V, 1)])
        c.on("tensor", lambda e: e.matmul(ps_a[0:1, 0:3], ones_col[:], red1[:, 0:3],
                                          start=True, stop=True),
             waits=[(V, c.val(V))], incs=[(Tn, 1)])
        c.wait("vector", [(Tn, c.val(Tn))])
        vchain([
            lambda e: e.tensor_scalar(mrow[0:1, 6:9], ps_a[0:1, 0:3],
                                      1.0 / (D * I), 1e-5, OP.mult, OP.max),
            lambda e: e.tensor_scalar_mul(mrow[0:1, 0:3], mrow[0:1, 6:9], 0.5),
            lambda e: e.tensor_scalar_mul(mrow[0:1, 3:6], mrow[0:1, 6:9], -0.5),
        ])
        c.on("tensor", lambda e: e.matmul(ps_b[:, 0:9], ones_row[:], mrow[:],
                                          start=True, stop=True),
             waits=[(V, c.val(V))], incs=[(Tn, 1)])
        c.on("vector", lambda e: e.tensor_copy(thr[:], ps_b[:, 0:9]),
             waits=[(Tn, c.val(Tn))], incs=[(V, 1)])
        V_thr = c.val(V)

        c.wait("vector", [(L, L_w)])
        vchain([lambda e: e.reciprocal(mu[:], lam[:])])
        for m in range(KD):
            sl = slice(128 * m, 128 * (m + 1))
            s16 = slice(16 * m, 16 * (m + 1))
            vchain([
                lambda e, m=m, sl=sl: e.tensor_scalar_mul(fF[:, sl], ones128[:], lam[:, m:m + 1]),
                lambda e, m=m, sl=sl: e.tensor_tensor_scan(
                    tRS[:, sl], fF[:, sl], fF[:, sl], 1.0, OP.mult, OP.bypass),
                lambda e, m=m: e.tensor_scalar_mul(fF[:, 128 * m:128 * m + 16], ones128[:, 0:16], mu[:, m:m + 1]),
                lambda e, m=m, s16=s16: e.tensor_tensor_scan(
                    tMU[:, s16], fF[:, 128 * m:128 * m + 16], fF[:, 128 * m:128 * m + 16],
                    1.0, OP.mult, OP.bypass),
                lambda e, m=m: e.memset(fF[:, 128 * m:128 * m + 1], 1.0),
                lambda e, m=m, s16=s16: e.tensor_copy(fF[:, 128 * m + 1:128 * m + 16], tMU[:, 16 * m:16 * m + 15]),
                lambda e, m=m: e.tensor_scalar_mul(fF[:, 128 * m:128 * m + 16],
                                                   fF[:, 128 * m:128 * m + 16], tfb[:, m:m + 1]),
                lambda e, m=m: e.tensor_copy(
                    tK16d[:, 128 * m:128 * (m + 1)].rearrange("p (r cc) -> p r cc", r=8),
                    fF[:, 128 * m:128 * m + 16].unsqueeze(1).broadcast_to([128, 8, 16])),
                lambda e, m=m: e.memset(fF[:, 128 * m + 64:128 * m + 65], 1.0),
                lambda e, m=m: e.tensor_copy(fF[:, 128 * m + 65:128 * (m + 1)],
                                             tRS[:, 128 * m:128 * m + 63]),
                lambda e, m=m: e.tensor_copy(
                    tR16[:, 128 * m:128 * (m + 1)].rearrange("p (r cc) -> p r cc", r=8),
                    fF[:, 128 * m + 64:128 * m + 80].unsqueeze(1).broadcast_to([128, 8, 16])),
                lambda e, m=m: e.tensor_copy(
                    tR32[:, 128 * m:128 * (m + 1)].rearrange("p (r cc) -> p r cc", r=4),
                    fF[:, 128 * m + 64:128 * m + 96].unsqueeze(1).broadcast_to([128, 4, 32])),
                lambda e, m=m: e.tensor_copy(
                    tR64[:, 128 * m:128 * (m + 1)].rearrange("p (r cc) -> p r cc", r=2),
                    fF[:, 128 * m + 64:128 * (m + 1)].unsqueeze(1).broadcast_to([128, 2, 64])),
                lambda e, m=m, sl=sl: e.tensor_scalar_mul(tK16[:, sl], tK16d[:, sl],
                                                          tRS[:, 128 * m + 15:128 * m + 16]),
                lambda e, m=m: e.tensor_scalar_mul(fF[:, 128 * m:128 * m + 16],
                                                   ones128[:, 0:16], tRS[:, 128 * m + 15:128 * m + 16]),
                lambda e, m=m: e.memset(fF[:, 128 * m + 16:128 * m + 32], 1.0),
                lambda e, m=m, sl=sl: e.scalar_tensor_tensor(
                    tK32[:, sl].rearrange("p (r cc) -> p r cc", r=4),
                    tK16[:, sl].rearrange("p (r cc) -> p r cc", r=4), 1.0,
                    fF[:, 128 * m:128 * m + 32].unsqueeze(1).broadcast_to([128, 4, 32]),
                    OP.mult, OP.mult),
                lambda e, m=m: e.tensor_scalar_mul(fF[:, 128 * m:128 * m + 32],
                                                   ones128[:, 0:32], tRS[:, 128 * m + 31:128 * m + 32]),
                lambda e, m=m: e.memset(fF[:, 128 * m + 32:128 * m + 64], 1.0),
                lambda e, m=m, sl=sl: e.scalar_tensor_tensor(
                    tK64[:, sl].rearrange("p (r cc) -> p r cc", r=2),
                    tK32[:, sl].rearrange("p (r cc) -> p r cc", r=2), 1.0,
                    fF[:, 128 * m:128 * m + 64].unsqueeze(1).broadcast_to([128, 2, 64]),
                    OP.mult, OP.mult),
                lambda e, m=m: e.tensor_scalar_mul(fF[:, 128 * m:128 * m + 64],
                                                   ones128[:, 0:64], tRS[:, 128 * m + 62:128 * m + 63]),
                lambda e, m=m: e.tensor_scalar_mul(fF[:, 128 * m + 64:128 * (m + 1)],
                                                   ones128[:, 0:64], mu[:, m:m + 1]),
                lambda e, m=m, sl=sl: e.tensor_mul(tKS[:, sl], tK64[:, sl], fF[:, sl]),
            ])
        V_tbl = c.val(V)

        v_x2_done = {-1: 0}
        v_proj_done = {-1: 0, -2: 0}
        tn_lnB = {}
        x_fmt_b = [x_fmt0, x_fmt1]
        y_pair_ev = {}
        s_ypair = {}
        tn_yintra = {}
        v_lvl_prev = 0
        s_trp_prev = 0
        for ci in range(NCH):
            x_fmt = x_fmt_b[ci % 2]
            w_ld = [(V, max(V_tbl, v_proj_done[ci - 2]))]
            c.dma("sync", lambda e, ci=ci, x_fmt=x_fmt: e.dma_start(
                out=x_fmt[:].rearrange("p (m t) -> p m t", m=KD),
                in_=x_fm_d[:, C * ci:C * (ci + 1)].rearrange("(m p) t -> p m t", p=128)),
                L, waits=w_ld)
            Lxf = c.val(L)
            c.dma("sync", lambda e, ci=ci: e.dma_start(
                out=x_tmt[:], in_=x_tm_d[C * ci:C * (ci + 1), :]), L,
                waits=[(V, v_x2_done[ci - 1])])
            Lx = c.val(L)

            w_sq = ([(Tn, tn_lnB[ci - 1])] if ci >= 1 else [])
            c.on("scalar", lambda e, x_fmt=x_fmt: e.activation(sq_fm[:], x_fmt[:], AF.Square),
                 waits=[(L, Lxf)] + w_sq, incs=[(S, 1)])
            c.wait("tensor", [(L, Lxf), (S, c.val(S)), (V, max(V_tbl, v_x2_done[ci - 1]))])
            for m in range(KD):
                c.on("tensor", lambda e, m=m, x_fmt=x_fmt: e.matmul(
                    ps_a[0:1, 0:128], ones_col[:], x_fmt[:, 128 * m:128 * (m + 1)],
                    start=(m == 0), stop=(m == KD - 1)))
            for m in range(KD):
                c.on("tensor", lambda e, m=m: e.matmul(
                    ps_b[0:1, 0:128], ones_col[:], sq_fm[:, 128 * m:128 * (m + 1)],
                    start=(m == 0), stop=(m == KD - 1)),
                    incs=([(Tn, 1)] if m == KD - 1 else []))
            tn_lnB[ci] = c.val(Tn)
            c.wait("vector", [(Tn, c.val(Tn))])
            vchain([
                lambda e: e.tensor_scalar_mul(srow[0:1, 0:128], ps_a[0:1, 0:128], 1.0 / D),
                lambda e: e.tensor_scalar_mul(tmp1[0:1, :], ps_b[0:1, 0:128], 1.0 / D),
                lambda e: e.tensor_mul(srow[0:1, 128:256], srow[0:1, 0:128], srow[0:1, 0:128]),
                lambda e: e.tensor_sub(tmp1[0:1, :], tmp1[0:1, :], srow[0:1, 128:256]),
            ])
            c.on("scalar", lambda e: e.activation(tmp1[0:1, :], tmp1[0:1, :], AF.Sqrt, bias=epsc[0:1, 0:1]),
                 waits=[(V, c.val(V))], incs=[(S, 1)])
            c.wait("vector", [(S, c.val(S))])
            vchain([lambda e: e.reciprocal(srow[0:1, 128:256], tmp1[0:1, :])])
            c.on("tensor", lambda e: e.matmul(ps_a[:, 0:256], ones_row[:], srow[:],
                                              start=True, stop=True),
                 waits=[(V, c.val(V))], incs=[(Tn, 1)])
            c.wait("vector", [(Tn, c.val(Tn))])
            vchain([
                lambda e, x_fmt=x_fmt: e.scalar_tensor_tensor(
                    x_fmt[:].rearrange("p (m t) -> p m t", m=KD),
                    x_fmt[:].rearrange("p (m t) -> p m t", m=KD), 1.0,
                    ps_a[:, 0:128].unsqueeze(1).broadcast_to([128, KD, 128]),
                    OP.mult, OP.subtract),
                lambda e, x_fmt=x_fmt: e.tensor_tensor(
                    x_fmt[:].rearrange("p (m t) -> p m t", m=KD),
                    x_fmt[:].rearrange("p (m t) -> p m t", m=KD),
                    ps_a[:, 128:256].unsqueeze(1).broadcast_to([128, KD, 128]),
                    OP.mult),
            ])
            V_xn = c.val(V)

            ev_done = {}
            for mi, (wmat, dst) in enumerate([(wk, k_fm), (wv, v_fm), (wr, r_fm)]):
                for m in range(KD):
                    gidx = mi * KD + m
                    bank = (ps_a, ps_b)[gidx % 2]
                    if gidx >= 2:
                        c.wait("tensor", [ev_done[gidx - 2]])
                    else:
                        c.wait("tensor", [(V, V_xn)])
                    for kk in range(KD):
                        c.on("tensor", lambda e, m=m, kk=kk, w=wmat, bank=bank, x_fmt=x_fmt: e.matmul(
                            bank[:, 0:128], w[:, kk * D + 128 * m:kk * D + 128 * (m + 1)],
                            x_fmt[:, 128 * kk:128 * (kk + 1)],
                            start=(kk == 0), stop=(kk == KD - 1)),
                            incs=([(Tn, 1)] if kk == KD - 1 else []))
                    if mi < 2:
                        w_kx = [(X, 16 * ci)] if (gidx == 0 and ci >= 1) else []
                        c.on("vector", lambda e, m=m, d=dst, bank=bank: e.tensor_copy(
                            d[:, 128 * m:128 * (m + 1)], bank[:, 0:128]),
                            waits=[(Tn, c.val(Tn))] + w_kx, incs=[(V, 1)])
                        ev_done[gidx] = (V, c.val(V))
                    else:
                        c.on("scalar", lambda e, m=m, bank=bank: e.activation(
                            r_fm[:, 128 * m:128 * (m + 1)], bank[:, 0:128], AF.Sigmoid),
                            waits=[(Tn, c.val(Tn))], incs=[(S, 1)])
                        ev_done[gidx] = (S, c.val(S))
            S_r = c.val(S)
            v_proj_done[ci] = c.val(V)

            var8_done = {}
            varRS_done = {}

            def emit_var8(pi, w_extra):
                csl = slice(128 * pi, 128 * (pi + 1))
                c.wait("vector", [(S, S_r), (V, v_proj_done[ci])] + w_extra)
                for li, (dst, src, tab) in enumerate([
                        (0, r_fm, tR16), (1, r_fm, tR32), (2, r_fm, tR64),
                        (4, k_fm, tK16d), (5, k_fm, tK16), (6, k_fm, tK32),
                        (7, k_fm, tK64), (8, k_fm, tKS)]):
                    c.on("vector", lambda e, dst=dst, s_=src, t_=tab, csl=csl: e.tensor_mul(
                        pvar[:, 128 * dst:128 * (dst + 1)], s_[:, csl], t_[:, csl]),
                        incs=([(V, 1)] if li == 7 else []))
                var8_done[pi] = c.val(V)

            def emit_varRS(pi, w_extra):
                csl = slice(128 * pi, 128 * (pi + 1))
                c.wait("vector", w_extra)
                c.on("vector", lambda e, csl=csl: e.tensor_mul(
                    pvar[:, 3 * 128:4 * 128], r_fm[:, csl], tRS[:, csl]),
                    incs=[(V, 1)])
                varRS_done[pi] = c.val(V)

            w0 = ([(Tn, tn_yintra[(ci - 1, H - 1)])]
                  if (ci - 1, H - 1) in tn_yintra else [])
            emit_var8(0, w0)
            emit_varRS(0, w0)
            v_merge_h1 = 0
            for pi in range(NP):
                csl = slice(128 * pi, 128 * (pi + 1))
                c.wait("tensor", [(V, var8_done[pi]), (S, s_trp_prev)])
                c.on("tensor", lambda e, csl=csl: e.transpose(ps_A[0][:], v_fm[:, csl], idf[:]))
                c.on("tensor", lambda e: e.transpose(ps_A[1][:], pvar[:, 8 * 128:9 * 128], idf[:]),
                     incs=[(Tn, 1)])
                c.on("scalar", lambda e: e.activation(v_tm[:], ps_A[0][:], AF.Copy),
                     waits=[(Tn, c.val(Tn))])
                c.on("scalar", lambda e: e.activation(ks_tm[:], ps_A[1][:], AF.Copy),
                     incs=[(S, 1)])
                s_trp_prev = c.val(S)
                perm0 = (2, 3, 0, 1)
                for li, (kvi, rvi) in enumerate([(4, 0), (5, 0), (6, 1), (7, 2)]):
                    if li == 2:
                        c.wait("tensor", [(S, s_trp_prev)])
                    c.on("tensor", lambda e, li=li, kvi=kvi, rvi=rvi: e.matmul(
                        ps_A[perm0[li]][:], pvar[0:64, 128 * kvi:128 * (kvi + 1)],
                        pvar[0:64, 128 * rvi:128 * (rvi + 1)],
                        start=True, stop=True),
                        incs=([(Tn, 1)] if li == 3 else []))
                tnA0 = c.val(Tn)
                h0 = 2 * pi
                w_g = [(Tn, tnA0)]
                key = (ci, h0 - 2) if h0 >= 2 else (ci - 1, H - 2)
                if key in tn_yintra:
                    w_g.append((Tn, tn_yintra[key]))
                c.wait("vector", w_g)
                asb0 = A_sb[0]
                c.on("vector", lambda e: e.tensor_mul(asb0[:], ps_A[2][:], mdiag[:]))
                c.on("vector", lambda e: e.copy_predicated(asb0[:], m16[:], ps_A[3][:]))
                c.on("vector", lambda e: e.copy_predicated(asb0[:], m32[:], ps_A[0][:]))
                c.on("vector", lambda e: e.copy_predicated(asb0[:], m64[:], ps_A[1][:]),
                     incs=[(V, 1)])
                c.wait("vector", [(V, c.val(V))])
                v_merge_h0 = c.val(V)
                h1banks = (ps_a, ps_b, ps_A[2], ps_A[3])
                for li, (kvi, rvi) in enumerate([(4, 0), (5, 0), (6, 1), (7, 2)]):
                    if li == 0:
                        c.wait("tensor", [(V, v_merge_h1)])
                    if li == 2:
                        c.wait("tensor", [(V, v_merge_h0)])
                    c.on("tensor", lambda e, li=li, kvi=kvi, rvi=rvi: e.matmul(
                        h1banks[li][:, 0:128], pvar[64:128, 128 * kvi:128 * (kvi + 1)],
                        pvar[64:128, 128 * rvi:128 * (rvi + 1)],
                        start=True, stop=True),
                        incs=([(Tn, 1)] if li == 3 else []))
                tnA1 = c.val(Tn)
                h1 = 2 * pi + 1
                w_g = [(Tn, tnA1)]
                key = (ci, h1 - 2) if h1 >= 2 else (ci - 1, H - 1)
                if key in tn_yintra:
                    w_g.append((Tn, tn_yintra[key]))
                c.wait("vector", w_g)
                asb1 = A_sb[1]
                c.on("vector", lambda e: e.tensor_mul(asb1[:], ps_a[:, 0:128], mdiag[:]))
                c.on("vector", lambda e: e.copy_predicated(asb1[:], m16[:], ps_b[:, 0:128]))
                c.on("vector", lambda e: e.copy_predicated(asb1[:], m32[:], ps_A[2][:]))
                c.on("vector", lambda e: e.copy_predicated(asb1[:], m64[:], ps_A[3][:]),
                     incs=[(V, 1)])
                c.wait("vector", [(V, c.val(V))])
                v_merge_h1 = c.val(V)
                if pi + 1 < NP:
                    emit_var8(pi + 1, [(Tn, tnA1)])
                for hh in range(2):
                    lo = hh * 64
                    prng = slice(lo, lo + 64)
                    asb = A_sb[hh]
                    if hh == 0:
                        prev_s = s_ypair.get((ci, pi - 1), s_ypair.get((ci - 1, NP - 1), 0))
                        c.wait("tensor", [(S, prev_s), (V, varRS_done[pi])])
                    else:
                        c.wait("tensor", [(V, v_merge_h1)])
                    c.on("tensor", lambda e, pi=pi, lo=lo, prng=prng: e.matmul(
                        ps_y[lo:lo + 64, :], S_all[prng, 64 * pi:64 * (pi + 1)],
                        pvar[prng, 3 * 128:4 * 128],
                        start=True, stop=False,
                        tile_position=(lo, lo)))
                    c.on("tensor", lambda e, lo=lo, asb=asb: e.matmul(
                        ps_y[lo:lo + 64, :], v_tm[:, lo:lo + 64],
                        asb[:],
                        start=False, stop=True,
                        tile_position=(0, lo)),
                        incs=[(Tn, 1)])
                    tn_yintra[(ci, 2 * pi + hh)] = c.val(Tn)
                tn_yh1 = c.val(Tn)
                prev = y_pair_ev.get((ci, pi - 1), y_pair_ev.get((ci - 1, NP - 1), 0))
                c.wait("tensor", [(V, prev)])
                for hh in range(2):
                    lo = hh * 64
                    c.on("tensor", lambda e, lo=lo: e.matmul(
                        ps_d[lo:lo + 64, :], ks_tm[:, lo:lo + 64],
                        v_tm[:, lo:lo + 64],
                        start=True, stop=True,
                        tile_position=(0, lo)),
                        incs=([(Tn, 1)] if lo == 64 else []))
                if pi + 1 < NP:
                    emit_varRS(pi + 1, [(Tn, tn_yh1)])
                c.on("scalar", lambda e, pi=pi: e.activation(
                        y_fm[:, 128 * pi:128 * (pi + 1)], ps_y[:], AF.Copy),
                     waits=[(Tn, tn_yh1)], incs=[(S, 1)])
                s_yev = c.val(S)
                c.wait("vector", [(Tn, c.val(Tn))])
                vchain([
                    lambda e, pi=pi: e.scalar_tensor_tensor(
                        S_all[:, 64 * pi:64 * (pi + 1)], S_all[:, 64 * pi:64 * (pi + 1)],
                        tRS[:, 128 * pi + 127:128 * (pi + 1)], ps_d[:],
                        OP.mult, OP.add),
                ])
                y_pair_ev[(ci, pi)] = c.val(V)
                s_ypair[(ci, pi)] = s_yev
            V_y = c.val(V)
            S_y = c.val(S)

            for n in range(2):
                bank = (ps_a, ps_b)[n]
                c.wait("tensor", [(V, V_y), (S, S_y)])
                for kk in range(KD):
                    c.on("tensor", lambda e, kk=kk, n=n, bank=bank: e.matmul(
                        bank[:], y_fm[:, 128 * kk:128 * (kk + 1)],
                        wo[:, kk * D + 512 * n:kk * D + 512 * (n + 1)],
                        start=(kk == 0), stop=(kk == KD - 1)),
                        incs=([(Tn, 1)] if kk == KD - 1 else []))
                w_o = ([(X, 16 * ci)] if ci >= 1 else []) + [(L, Lx)] if n == 0 else []
                c.on("vector", lambda e, n=n, bank=bank: e.tensor_tensor(
                        x2_sb[:, 512 * n:512 * (n + 1)], bank[:],
                        x_tmt[:, 512 * n:512 * (n + 1)], OP.add),
                     waits=[(Tn, c.val(Tn))] + w_o, incs=[(V, 1)])
            v_x2_done[ci] = c.val(V)
            c.dma("gpsimd", lambda e, ci=ci: e.dma_start(
                out=x2_scr[C * ci:C * (ci + 1), :], in_=x2_sb[:]),
                X, waits=[(V, c.val(V))])
        V_endA = c.val(V)
        Tn_endA = c.val(Tn)

        G_endA = c.val(G)
        for j in range(16):
            m, kt = j // 8, j % 8
            tile = (wgT_d, wuT_d)[m][128 * kt:128 * (kt + 1), :]
            w = [(V, V_endA), (G, G_endA), (S, c.val(S))] if j < 2 else [(V, w2a_v0 + 3 * (j - 1))]
            c.dma("sync", lambda e, t=tile, b=j % 2: e.dma_start(out=stag[b][:], in_=t),
                 L, waits=w)
            if j == 0:
                w2a_v0 = c.val(V)
            dsl = (wgq, wuq)[m][:, kt * I:(kt + 1) * I]
            w_first = [(Tn, Tn_endA)] if j == 0 else []
            c.on("vector", lambda e, b=j % 2, m=m: e.tensor_scalar(
                    wqa[:], stag[b][:], thr[:, m:m + 1], None, OP.is_ge),
                 waits=[(L, c.val(L))] + w_first, incs=[(V, 1)])
            c.on("vector", lambda e, b=j % 2, m=m: e.tensor_scalar(
                    wqb[:], stag[b][:], thr[:, 3 + m:4 + m], None, OP.is_le),
                 incs=[(V, 1)])
            c.on("vector", lambda e, d=dsl: e.tensor_sub(d, wqa[:], wqb[:]),
                 incs=[(V, 1)])
        V_wq = c.val(V)

        SUM, SUMSQ, MEAN, MS, M2, VAR, SD, RSTD, AMAX, C1, RC, S1, RS1, AG, AU, ZMX, C2, S2 = range(18)
        AG2, AU2 = 18, 19
        v_xn_done = {-1: V_wq, -2: V_wq}
        v_var = {}
        s_sqrt = {}
        q1done = {}
        lx_ld = {}
        tn_q1tr = {}
        v_q2c = {-1: 0}
        v_zt = {(-1, 0): 0, (-1, 1): 0}
        s_gu_done = {}
        last_s_bank = {bk: 0 for bk in range(6)}
        vtq = {}

        def emit_load(j):
            if j >= NT:
                return
            c.dma("sync", lambda e, b=j % 2, j=j: e.dma_start(
                    out=x2t[b][:], in_=x2_scr[128 * j:128 * (j + 1), :]),
                 L, waits=[(V, v_xn_done[j - 2])] + [(X, 16 * (j + 1))])
            lx_ld[j] = c.val(L)

        def emit_Apre(j):
            if j >= NT:
                return
            b = j % 2
            c.wait("vector", [(L, lx_ld[j])])
            vchain([
                lambda e, b=b: e.reduce_sum(sm[:, SUM:SUM + 1], x2t[b][:], axis=AX),
                lambda e, b=b: e.tensor_mul(sqt[:], x2t[b][:], x2t[b][:]),
                lambda e: e.reduce_sum(sm[:, SUMSQ:SUMSQ + 1], sqt[:], axis=AX),
                lambda e: e.tensor_scalar_mul(sm[:, MEAN:MEAN + 1], sm[:, SUM:SUM + 1], 1.0 / D),
                lambda e: e.tensor_scalar_mul(sm[:, MS:MS + 1], sm[:, SUMSQ:SUMSQ + 1], 1.0 / D),
                lambda e: e.tensor_mul(sm[:, M2:M2 + 1], sm[:, MEAN:MEAN + 1], sm[:, MEAN:MEAN + 1]),
                lambda e: e.tensor_sub(sm[:, VAR:VAR + 1], sm[:, MS:MS + 1], sm[:, M2:M2 + 1]),
            ])
            v_var[j] = c.val(V)

        def emit_Sqrt(j):
            if j >= NT:
                return
            c.on("scalar", lambda e: e.activation(sm[:, SD:SD + 1], sm[:, VAR:VAR + 1],
                                                  AF.Sqrt, bias=epsc[:, 0:1]),
                 waits=[(V, v_var[j])], incs=[(S, 1)])
            s_sqrt[j] = c.val(S)

        def emit_Apost(j):
            if j >= NT:
                return
            b = j % 2
            ag, au = (AG, AU) if b == 0 else (AG2, AU2)
            c.wait("vector", [(S, s_sqrt[j])])
            vchain([
                lambda e: e.reciprocal(sm[:, RSTD:RSTD + 1], sm[:, SD:SD + 1]),
                lambda e, b=b: e.tensor_scalar(
                    xnt[:], x2t[b][:], sm[:, MEAN:MEAN + 1], sm[:, RSTD:RSTD + 1],
                    OP.subtract, OP.mult),
            ])
            v_xn_done[j] = c.val(V)
            vchain([
                lambda e: e.reduce_max(sm[:, AMAX:AMAX + 1], xnt[:], axis=AX,
                                       apply_absolute_value=True),
                lambda e: e.tensor_scalar_max(sm[:, C1:C1 + 1], sm[:, AMAX:AMAX + 1], 1e-5),
                lambda e: e.reciprocal(sm[:, RC:RC + 1], sm[:, C1:C1 + 1]),
                lambda e: e.tensor_scalar_mul(sm[:, S1:S1 + 1], sm[:, RC:RC + 1], 127.0),
                lambda e: e.tensor_scalar_mul(sm[:, RS1:RS1 + 1], sm[:, C1:C1 + 1], 1.0 / 127.0),
            ])
            c.wait("vector", [(S, s_gu_done.get(j - 2, 0))])
            vchain([
                lambda e, ag=ag: e.tensor_mul(sm[:, ag:ag + 1], sm[:, RS1:RS1 + 1], thr[:, 6:7]),
                lambda e, au=au: e.tensor_mul(sm[:, au:au + 1], sm[:, RS1:RS1 + 1], thr[:, 7:8]),
                lambda e: e.tensor_scalar_mul(u1t[:], xnt[:], sm[:, S1:S1 + 1]),
                lambda e: e.tensor_copy(fl1[:], u1t[:]),
            ])
            c.wait("vector", [(Tn, (lambda jj=j - 2: pred.get(("t2", jj), 0)))])
            vchain([
                lambda e: e.scalar_tensor_tensor(t1a[:], u1t[:], -0.5, fl1[:], OP.add, OP.is_ge),
                lambda e: e.scalar_tensor_tensor(t1b[:], u1t[:], 0.5, fl1[:], OP.add, OP.is_lt),
                lambda e: e.tensor_add(u1t[:], fl1[:], t1a[:]),
            ])
            c.wait("vector", [(Tn, tn_q1tr.get(j - 1, 0))])
            vchain([
                lambda e: e.tensor_sub(q1tm[:], u1t[:], t1b[:]),
            ])
            q1done[j] = c.val(V)

        def emit_T2(j):
            for hf in range(2):
                qb = (q2tm, q2tm2)[hf]
                for jj in range(16):
                    bk = jj % 6
                    c.on("tensor", lambda e, jj=jj, bk=bk, qb=qb: e.transpose(
                            ps_t[bk][:], qb[:, 128 * jj:128 * (jj + 1)], identb[:]),
                         waits=[(V, vtq[(j, hf)]), (S, last_s_bank[bk])], incs=[(Tn, 1)])
                    bb = jj // 8
                    nsp = 4 * j + 2 * hf + bb
                    w_sp = [(P, 16 * (nsp - 1))] if (jj % 8 == 0 and nsp >= 2) else []
                    c.on("scalar", lambda e, jj=jj, bb=bb, bk=bk: e.activation(
                            q2pp[bb][:, 128 * (jj % 8):128 * (jj % 8 + 1)], ps_t[bk][:], AF.Copy),
                         waits=[(Tn, c.val(Tn))] + w_sp, incs=[(S, 1)])
                    last_s_bank[bk] = c.val(S)
                    if jj % 8 == 7:
                        c.dma("gpsimd", lambda e, j=j, hf=hf, bb=bb: e.dma_start(
                                out=q2_scr[j][:, (2 * hf + bb) * 1024:(2 * hf + bb + 1) * 1024],
                                in_=q2pp[bb][:]),
                             P, waits=[(S, c.val(S))])
            pred[("t2", j)] = c.val(Tn)

        emit_load(0)
        emit_load(1)
        emit_Apre(0)
        emit_Sqrt(0)
        emit_Apost(0)
        emit_Apre(1)
        for i in range(NT):
            b = i % 2
            ag, au = (AG, AU) if b == 0 else (AG2, AU2)
            emit_load(i + 2)
            for k in range(KD):
                bk = k % 6
                c.on("tensor", lambda e, k=k, bk=bk: e.transpose(
                        ps_t[bk][:], q1tm[:, 128 * k:128 * (k + 1)], identb[:]),
                     waits=[(V, q1done[i]), (S, last_s_bank[bk])], incs=[(Tn, 1)])
                c.on("scalar", lambda e, k=k, bk=bk: e.activation(
                        q1fm[:, 128 * k:128 * (k + 1)], ps_t[bk][:], AF.Copy),
                     waits=[(Tn, c.val(Tn))], incs=[(S, 1)])
                last_s_bank[bk] = c.val(S)
            tn_q1tr[i] = c.val(Tn)
            S_q1 = c.val(S)
            emit_Sqrt(i + 1)
            emit_Apost(i + 1)
            s_evac = {}
            grp = 0
            for hhf in range(2):
                for part in range(2):
                    if hhf == 1 and part == 1 and i >= 1:
                        emit_T2(i - 1)
                    wsrc = (wgq, wuq)[part]
                    alpha_col = (ag, au)[part]
                    func = (AF.Silu, AF.Copy)[part]
                    dst_sb = (gt, ut)[part]
                    for ns in range(4):
                        bank = (ps_a, ps_b)[grp % 2]
                        w_bank = [(S, s_evac[grp - 2])] if grp >= 2 else []
                        c.wait("tensor", [(S, S_q1)] + w_bank)
                        for k in range(KD):
                            c.on("tensor", lambda e, k=k, hf=hhf, ns=ns, w=wsrc, bank=bank:
                                 e.matmul(bank[:],
                                          q1fm[:, 128 * k:128 * (k + 1)],
                                          w[:, k * I + hf * HALF + 512 * ns: k * I + hf * HALF + 512 * (ns + 1)],
                                          start=(k == 0), stop=(k == KD - 1)),
                                 incs=([(Tn, 1)] if k == KD - 1 else []))
                        c.on("scalar", lambda e, bank=bank, ns=ns, ac=alpha_col, fn=func, d=dst_sb:
                             e.activation(d[:, 512 * ns:512 * (ns + 1)], bank[:], fn,
                                          scale=sm[:, ac:ac + 1]),
                             waits=[(Tn, c.val(Tn)), (G, c.val(G))], incs=[(S, 1)])
                        s_evac[grp] = c.val(S)
                        if part == 1:
                            w_z = [(V, v_zt[(i - 1, hhf)])] if ns == 0 else []
                            c.on("gpsimd", lambda e, hf=hhf, ns=ns: e.tensor_mul(
                                    zt[:, hf * HALF + 512 * ns:hf * HALF + 512 * (ns + 1)],
                                    gt[:, 512 * ns:512 * (ns + 1)], ut[:, 512 * ns:512 * (ns + 1)]),
                                 waits=[(S, c.val(S))] + w_z, incs=[(G, 1)])
                        grp += 1
            s_gu_done[i] = c.val(S)
            emit_Apre(i + 2)
            c.wait("vector", [(G, c.val(G))])
            vchain([
                lambda e: e.reduce_max(zm2[:, 0:1], zt[:, 0:HALF],
                                       axis=AX, apply_absolute_value=True),
                lambda e: e.reduce_max(zm2[:, 1:2], zt[:, HALF:I],
                                       axis=AX, apply_absolute_value=True),
                lambda e: e.tensor_max(sm[:, ZMX:ZMX + 1], zm2[:, 0:1], zm2[:, 1:2]),
                lambda e: e.tensor_scalar_max(sm[:, C2:C2 + 1], sm[:, ZMX:ZMX + 1], 1e-5),
                lambda e: e.reciprocal(sm[:, RC:RC + 1], sm[:, C2:C2 + 1]),
                lambda e: e.tensor_scalar_mul(sm[:, S2:S2 + 1], sm[:, RC:RC + 1], 127.0),
                lambda e, i=i: e.tensor_scalar_mul(s2col[:, i:i + 1], sm[:, C2:C2 + 1], 1.0 / 127.0),
            ])
            for hf in range(2):
                qb = (q2tm, q2tm2)[hf]
                vchain([
                    lambda e, hf=hf: e.tensor_scalar_mul(u2t[:], zt[:, hf * HALF:(hf + 1) * HALF],
                                                         sm[:, S2:S2 + 1]),
                ])
                v_zt[(i, hf)] = c.val(V)
                vchain([
                    lambda e: e.tensor_copy(fl2[:], u2t[:]),
                    lambda e: e.scalar_tensor_tensor(t2a[:], u2t[:], -0.5, fl2[:], OP.add, OP.is_ge),
                ])
                c.wait("vector", [(Tn, (lambda ii=i: pred.get(("t2", ii - 1), 0)))])
                vchain([
                    lambda e, qb=qb: e.tensor_add(qb[:], fl2[:], t2a[:]),
                    lambda e: e.scalar_tensor_tensor(t2a[:], u2t[:], 0.5, fl2[:], OP.add, OP.is_lt),
                    lambda e, qb=qb: e.tensor_sub(qb[:], qb[:], t2a[:]),
                ])
                vtq[(i, hf)] = c.val(V)
            v_q2c[i] = c.val(V)
        emit_T2(NT - 1)
        V_endB = c.val(V)
        Tn_endB = c.val(Tn)
        G_endB = c.val(G)

        for j in range(8):
            kt = j
            tile = tile_wd(kt)
            w = [(V, V_endB)] if j == 0 else ([(V, w2b_v0 + 3 * (j - 1))] if j >= 2
                 else [(P, 16 * 4 * NT)])
            c.dma("sync", lambda e, t=tile, b=j % 2: e.dma_start(
                    out=stag[b][:].rearrange("p (a d) -> p a d", a=4), in_=t),
                 L, waits=w)
            if j == 0:
                w2b_v0 = c.val(V)
            dsl = wdq[:, (4 * kt) * D:(4 * kt + 4) * D]
            w_first = [(Tn, Tn_endB), (G, G_endB), (S, c.val(S))] if j == 0 else []
            c.on("vector", lambda e, b=j % 2: e.tensor_scalar(
                    wqa[:], stag[b][:], thr[:, 2:3], None, OP.is_ge),
                 waits=[(L, c.val(L))] + w_first, incs=[(V, 1)])
            c.on("vector", lambda e, b=j % 2: e.tensor_scalar(
                    wqb[:], stag[b][:], thr[:, 5:6], None, OP.is_le),
                 incs=[(V, 1)])
            c.on("vector", lambda e, d=dsl: e.tensor_sub(d, wqa[:], wqb[:]),
                 incs=[(V, 1)])
        VC = c.val(V)

        vc_evac = {}
        xbase = c.val(X)
        for i in range(NT):
            b = i % 3
            bx = i % 2
            w_ld = [(V, vc_evac[i - 3])] if i >= 3 else [(V, VC)]
            c.dma("sync", lambda e, b=b, i=i: e.dma_start(out=q2ld[b][:], in_=q2_scr[i]),
                 L, waits=w_ld + [(P, 64 * (i + 1))])
            w_lx = [(V, vc_evac[i - 2])] if i >= 2 else []
            c.dma("sync", lambda e, bx=bx, i=i: e.dma_start(
                    out=x2t[bx][:], in_=x2_scr[128 * i:128 * (i + 1), :]), L, waits=w_lx)
            Lx = c.val(L)
            vchain([
                lambda e, i=i: e.tensor_mul(sm[:, AG:AG + 1], s2col[:, i:i + 1], thr[:, 8:9]),
            ])
            for n in range(2):
                bank = (ps_a, ps_b)[n]
                c.wait("tensor", [(L, Lx), (V, c.val(V))])
                for seg in range(KI):
                    c.on("tensor", lambda e, seg=seg, n=n, b=b, bank=bank: e.matmul(
                        bank[:], q2ld[b][:, 128 * seg:128 * (seg + 1)],
                        wdq[:, seg * D + 512 * n: seg * D + 512 * (n + 1)],
                        start=(seg == 0), stop=(seg == KI - 1)),
                         incs=([(Tn, 1)] if seg == KI - 1 else []))
                w_out = [(X, xbase + 16 * i)] if (n == 0 and i >= 1) else []
                c.on("vector", lambda e, n=n, bx=bx, bank=bank: e.scalar_tensor_tensor(
                        outt[:, 512 * n:512 * (n + 1)], bank[:], sm[:, AG:AG + 1],
                        x2t[bx][:, 512 * n:512 * (n + 1)], OP.mult, OP.add),
                     waits=[(Tn, c.val(Tn))] + w_out, incs=[(V, 1)])
            vc_evac[i] = c.val(V)
            c.dma("gpsimd", lambda e, i=i: e.dma_start(
                out=out_d[128 * i:128 * (i + 1), :], in_=outt[:]),
                X, waits=[(V, c.val(V))])
        c.wait("sync", [(X, c.val(X)), (P, c.val(P)), (L, c.val(L))])

    sems = {name: nc.alloc_semaphore(name)
            for name in ["V", "S", "T", "G", "L", "O", "P", "F", "X"]}
    with nc.Block() as block:
        for name in ENGS:
            def make(n):
                def g(eng):
                    schedule(Ctx(sems, n, eng))
                return g
            getattr(block, name)(make(name))
    return nc


def kernel(L0_attn_x, L0_attn_Wk, L0_attn_Wv, L0_attn_Wr, L0_attn_Wo,
           L0_attn_time_decay, L0_attn_time_first,
           L0_ffn_w_gate, L0_ffn_w_up, L0_ffn_w_down):
    x = np.ascontiguousarray(np.asarray(L0_attn_x, dtype=np.float32))
    wkT = np.ascontiguousarray(np.asarray(L0_attn_Wk, np.float32).T)
    wvT = np.ascontiguousarray(np.asarray(L0_attn_Wv, np.float32).T)
    wrT = np.ascontiguousarray(np.asarray(L0_attn_Wr, np.float32).T)
    woT = np.ascontiguousarray(np.asarray(L0_attn_Wo, np.float32).T)
    wgT = np.ascontiguousarray(np.asarray(L0_ffn_w_gate, np.float32).T)
    wuT = np.ascontiguousarray(np.asarray(L0_ffn_w_up, np.float32).T)
    wdT = np.ascontiguousarray(np.asarray(L0_ffn_w_down, np.float32).T)
    lam_pc = np.ascontiguousarray(
        np.asarray(L0_attn_time_decay, np.float32).reshape(-1).reshape(KD, 128).T)
    tf_pc = np.ascontiguousarray(
        np.asarray(L0_attn_time_first, np.float32).reshape(-1).reshape(KD, 128).T)

    jj = np.arange(128)
    s_, t_ = np.meshgrid(jj, jj, indexing="ij")
    mdiag = ((s_ // 16 == t_ // 16) & (s_ <= t_)).astype(np.float32)
    m16 = (t_ // 16 - s_ // 16 == 1).astype(np.uint8)
    m32 = ((t_ // 32 - s_ // 32 == 1) & (t_ // 16 - s_ // 16 >= 2)).astype(np.uint8)
    m64 = ((t_ // 64 - s_ // 64 == 1) & (t_ // 32 - s_ // 32 >= 2)).astype(np.uint8)
    idf = np.eye(128, dtype=np.float32)
    idb = np.eye(128, dtype=ml_dtypes.bfloat16)

    nc = build_nc()
    in_maps = []
    for core in range(8):
        xi = np.ascontiguousarray(x[core])
        in_maps.append({
            "x_tm": xi,
            "x_fm": np.ascontiguousarray(xi.T),
            "wkT": wkT, "wvT": wvT, "wrT": wrT, "woT": woT,
            "wgT": wgT, "wuT": wuT, "wdT": wdT,
            "lam": lam_pc, "tf": tf_pc,
            "idf": idf, "idb": idb,
            "mdiag": mdiag, "m16": m16, "m32": m32, "m64": m64,
        })
    res = run_bass_kernel_spmd(nc, in_maps, core_ids=list(range(8)))
    out = np.stack([res.results[i]["out"] for i in range(8)], axis=0)
    return out
